# version 10
# baseline (speedup 1.0000x reference)
import numpy as np
from contextlib import ExitStack

import concourse.bass as bass
import concourse.mybir as mybir
from concourse.bass_utils import run_bass_kernel_spmd

F32 = mybir.dt.float32
BF16 = mybir.dt.bfloat16
AF = mybir.ActivationFunctionType
ALU = mybir.AluOpType
AX = mybir.AxisListType

NT = 16
NC_D = 8
SQ01 = float(np.sqrt(0.1))

ALL_LAYERS = ("F0", "A0", "F0", "F1", "A1", "F1")


def _ap3(t, off, dims):
    return bass.AP(t, off, dims)


def _clean(toks):
    return tuple(x for x in toks if x is not None)


class MK:
    DMA_SEMS = ("dX", "dW2", "dWa", "dWb", "dWc", "dWd", "dWv", "dWo", "dQB", "dPB", "dCn", "dSt0", "dSt1", "dSt2", "dSt3", "dStW2")

    def __init__(self, nc, layers, tag=""):
        self.nc = nc
        self.layers = layers
        self.tag = tag
        self.prog = {e: [] for e in ("tensor", "vector", "scalar", "gpsimd", "sync")}
        self.cnt = {}
        self.semh = {}
        self.hiwater = {e: {} for e in self.prog}
        self.guards = {}

    def sem(self, name, ctx):
        self.semh[name] = ctx.enter_context(self.nc.semaphore(self.tag + name))
        self.cnt[name] = 0

    def raw(self, eng, fn):
        self.prog[eng].append(fn)

    def wait(self, eng, tok):
        if tok is None:
            return
        sname, val, producer = tok
        if producer == eng:
            return
        if self.hiwater[eng].get(sname, 0) >= val:
            return
        self.hiwater[eng][sname] = val
        h = self.semh[sname]
        self.raw(eng, lambda e, h=h, v=val: e.wait_ge(h, v))

    def wait_force(self, eng, tok):
        if tok is None:
            return
        sname, val, producer = tok
        if self.hiwater[eng].get(sname, 0) >= val:
            return
        self.hiwater[eng][sname] = val
        h = self.semh[sname]
        self.raw(eng, lambda e, h=h, v=val: e.wait_ge(h, v))

    AUTO_CHAIN = {"vector": "V", "scalar": "S"}

    def op(self, eng, method, *args, waits=(), fwaits=(), inc=None, **kw):
        auto = self.AUTO_CHAIN.get(eng)
        if auto is not None and method != "dma_start":
            self.wait_force(eng, (auto, self.cnt[auto], None))
        for t in waits:
            self.wait(eng, t)
        for t in fwaits:
            self.wait_force(eng, t)
        inc_list = []
        tok = None
        if inc is not None:
            name, k = (inc, 1) if isinstance(inc, str) else inc
            self.cnt[name] += k
            prod = None if name in self.DMA_SEMS else eng
            tok = (name, self.cnt[name], prod)
            inc_list = [(self.semh[name], k)]
        if auto is not None and method != "dma_start":
            if inc is None:
                self.cnt[auto] += 1
                tok = (auto, self.cnt[auto], eng)
                inc_list = [(self.semh[auto], 1)]
            else:
                assert inc == auto or (isinstance(inc, tuple) and inc[0] == auto), \
                    f"{eng} op with inc {inc}"

        def cl(e, method=method, args=args, kw=kw, inc_list=inc_list):
            ins = getattr(e, method)(*args, **kw)
            for (h, k) in inc_list:
                ins.then_inc(h, k)
        self.raw(eng, cl)
        return tok

    def dma(self, queue, out, in_, waits=(), sem="dX"):
        return self.op(queue, "dma_start", out, in_, waits=waits, inc=(sem, 16))

    def use(self, buf):
        return list(self.guards.get(buf, ()))

    def setg(self, buf, *toks):
        self.guards[buf] = [t for t in toks if t is not None]

    def use2(self, name, idx):
        return self.guards.get((name, idx), None)

    def set2(self, name, idx, tok):
        self.guards[(name, idx)] = tok

    def bank_init(self, n):
        prev = self.use("p0p1")
        return {i: (prev[i] if i < len(prev) else (prev[-1] if prev else None))
                for i in range(n)}

    def alloc(self, ctx, has_l1=True):
        nc = self.nc
        sb = lambda n, s, d: ctx.enter_context(nc.sbuf_tensor(self.tag + "sb_" + n, s, d))
        ps = lambda n, s, d: ctx.enter_context(nc.psum_tensor(self.tag + n, s, d))
        self.xr = sb("xr", [128, NT * 1024], F32)
        self.xnt = sb("xnt", [128, NC_D * 2048], BF16)
        self.ar1 = sb("ar1", [128, 32768], BF16)
        self.ar2 = sb("ar2", [128, 16384], BF16)
        self.ar3 = sb("ar3", [128, 4096], BF16)
        self.ident = sb("ident", [128, 128], BF16)
        self.onesb = sb("onesb", [1, 512], BF16)
        self.onesf = sb("onesf", [1, 128], F32)
        self.rwT = sb("rwT", [128, 64], BF16)
        self.gateT = sb("gateT", [128, 64], BF16)
        self.esc_row = sb("esc_row", [1, 8], F32)
        self.eb_row = sb("eb_row", [1, 8], F32)
        self.scale_half = sb("scale_half", [128, 8], F32)
        self.biasb = sb("biasb", [128, 8], F32)
        self.comb = sb("comb", [128, NT * 8], F32)
        self.ln_s1 = sb("ln_s1", [128, 16], F32)
        self.ln_ssq = sb("ln_ssq", [128, 16], F32)
        self.ln_m = sb("ln_m", [128, 16], F32)
        self.ln_t = sb("ln_t", [128, 16], F32)
        self.ln_lt = sb("ln_lt", [128, 16], F32)
        self.ln_rstd = sb("ln_rstd", [128, 16], F32)
        self.rsc = sb("rsc", [128, 64], F32)
        self.xn_scr = sb("xn_scr", [128, 1024], BF16)
        self.rrow = sb("rrow", [1, 512], F32)
        self.bias8 = sb("bias8", [128, 1], F32)

        self.p0 = ps("p0", [128, 1024], F32)
        self.p1 = ps("p1", [128, 1024], F32)
        self.p2 = ps("p2", [128, 1024], F32)
        self.px = ps("px", [128, 512], F32)
        self.pt = ps("pt", [128, 1024], BF16)
        self.A_banks = [(self.p0, 0), (self.p0, 512), (self.p1, 0), (self.p1, 512)]
        for s in self.DMA_SEMS + ("T", "S", "V"):
            self.sem(s, ctx)

    def w_cast_load(self, dst2d, src2d, cols_per_chunk, waits=(), sem="dW2"):
        dst3 = dst2d.rearrange("p (c f) -> p c f", f=cols_per_chunk)
        src3 = src2d.rearrange("(c p) f -> p c f", p=128)
        return self.dma("gpsimd", dst3, src3, waits=_clean(waits), sem=sem)

    def dma_done(self, s):
        return (s, self.cnt[s], None)

    def emit_prologue(self, d, has_f0, has_f1):
        xin3 = d["xin"].rearrange("(c p) f -> p c f", p=128)
        xr3 = self.xr[:, :].rearrange("p (c f) -> p c f", f=1024)
        for i in range(4):
            self.dma("sync", xr3[:, i * 4:(i + 1) * 4, :],
                     xin3[:, i * 4:(i + 1) * 4, :])
        self.dma("sync", self.ident[:, :], d["ident"][:, :])
        self.dma("sync", self.onesb[:, :], d["ones_bf"][:, :])
        self.dma("sync", self.onesf[:, :], d["ones_f"][:, :])
        if has_f0:
            self.w_cast_load(self.rwT[:, :], d["rw_t"], 8, sem="dCn")
            self.dma("sync", self.esc_row[:, :], d["esc_row"][:, :])
        if has_f1:
            self.w_cast_load(self.gateT[:, :], d["gate_t"], 8, sem="dCn")
            self.dma("sync", self.eb_row[:, :], d["eb_row"][:, :])
        dx_all = self.dma_done("dX")
        dcn_all = self.dma_done("dCn")
        t_esc = t_eb = None
        if has_f0:
            tb = self.op("tensor", "matmul", self.px[:, 0:8], self.onesf[0:1, 0:128],
                         self.esc_row[0:1, 0:8], start=True, stop=True,
                         waits=(dx_all, dcn_all), inc="T")
            t_esc = self.op("vector", "tensor_scalar", self.scale_half[:, :],
                            self.px[:, 0:8], 0.5, None, ALU.mult,
                            waits=(tb,), inc="V")
        if has_f1:
            tb3 = self.op("tensor", "matmul", self.px[:, 8:16], self.onesf[0:1, 0:128],
                          self.eb_row[0:1, 0:8], start=True, stop=True,
                          waits=_clean((dx_all, dcn_all, t_esc)), inc="T")
            t_eb = self.op("vector", "tensor_copy", self.biasb[:, :], self.px[:, 8:16],
                           waits=(tb3,), inc="V")
        self.bias8_tok = self.op("vector", "memset", self.bias8[:, :], -8.0, inc="V")
        self.setg("px", t_esc, t_eb)
        self.setg("xr", dx_all)
        self.t_consts = _clean((dx_all, dcn_all, t_esc, t_eb))

    def emit_ln(self):
        v, s, t = "vector", "scalar", "tensor"
        xr = self.xr
        xr_guard = _clean(self.use("xr") + self.use("xn_scr"))
        s1_tok = None
        ar3_dummy = self.ar3[:, 0:1024]
        ar3_prior = self.use("ar3")
        for tile in range(NT):
            xt = xr[:, tile * 1024:(tile + 1) * 1024]
            s1_tok = self.op(s, "activation", ar3_dummy, xt, AF.Copy,
                             accum_out=self.ln_s1[:, tile:tile + 1],
                             waits=_clean(tuple(xr_guard) + tuple(ar3_prior)) if tile == 0 else (),
                             inc="S")
            self.op(v, "scalar_tensor_tensor", self.xn_scr[:, 0:1024],
                    xt, 1.0, xt, ALU.mult, ALU.mult,
                    accum_out=self.ln_ssq[:, tile:tile + 1],
                    waits=_clean(tuple(self.use("xn_scr")) + tuple(xr_guard)) if tile == 0 else ())
        self.guards.setdefault("ar3", []).append(s1_tok)
        self.op(v, "tensor_scalar", self.ln_t[:, :16], self.ln_ssq[:, :16],
                1.0 / 1024, None, ALU.mult, waits=(s1_tok,))
        t_lnm = self.op(v, "tensor_scalar", self.ln_m[:, :16], self.ln_s1[:, :16],
                        1.0 / 1024, None, ALU.mult, waits=(s1_tok,), inc="V")
        self.op(v, "tensor_mul", self.rsc[:, 0:16], self.ln_m[:, :16], self.ln_m[:, :16])
        self.op(v, "tensor_sub", self.ln_t[:, :16], self.ln_t[:, :16], self.rsc[:, 0:16])
        tv = self.op(v, "tensor_scalar", self.ln_t[:, :16], self.ln_t[:, :16],
                     1e-5, None, ALU.add, inc="V")
        self.op(s, "activation", self.ln_lt[:, :16], self.ln_t[:, :16], AF.Ln,
                waits=(tv,))
        ts = self.op(s, "activation", self.ln_rstd[:, :16], self.ln_lt[:, :16],
                     AF.Exp, scale=-0.5, inc="S")
        prev_tt = None
        prev_evac = None
        pt_guard = self.use("pt")
        xs = self.xn_scr[:, 0:1024]
        for tile in range(NT):
            xt = xr[:, tile * 1024:(tile + 1) * 1024]
            tv = self.op(v, "tensor_scalar", xs, xt,
                         self.ln_m[:, tile:tile + 1], self.ln_rstd[:, tile:tile + 1],
                         ALU.subtract, ALU.mult,
                         waits=_clean((ts, prev_tt)), fwaits=(t_lnm,), inc="V")
            tt = None
            for j in range(NC_D):
                tt = self.op(t, "transpose", self.pt[:, j * 128:(j + 1) * 128],
                             xs[:, j * 128:(j + 1) * 128], self.ident[:, :],
                             waits=_clean([tv, prev_evac, ("dX", 16, None)] +
                                          (pt_guard if tile == 0 else [])) if j == 0 else (),
                             inc="T" if j == NC_D - 1 else None)
            prev_tt = tt
            dst = _ap3(self.xnt, tile * 128, [[NC_D * 2048, 128], [2048, NC_D], [1, 128]])
            src = self.pt[:, 0:1024].rearrange("p (c f) -> p c f", f=128)
            prev_evac = self.op(v, "tensor_copy", dst, src, waits=(tt,), inc="V")
        self.setg("xn_scr", prev_tt)
        self.setg("pt", prev_evac)
        return prev_evac

    def emit_f0(self, d, experts=True, first=True):
        t, v, s = "tensor", "vector", "scalar"
        xnt_ok = self.emit_ln()
        prev_te = None
        pxg = self.use("px")
        comb_guard = self.use("comb")
        last_router = None
        for tile in range(NT):
            off = (tile % 4) * 128
            pxr = self.px[:, off:off + 8]
            wts = _clean([xnt_ok, prev_te] + (pxg if tile == 0 else []) +
                         list(self.t_consts))
            tm = None
            for c in range(NC_D):
                tm = self.op(t, "matmul", pxr,
                             self.xnt[:, c * 2048 + tile * 128: c * 2048 + tile * 128 + 128],
                             self.rwT[:, c * 8:(c + 1) * 8],
                             start=(c == 0), stop=(c == NC_D - 1),
                             waits=wts if c == 0 else (),
                             inc="T" if c == NC_D - 1 else None)
            rs = self.rsc
            rmx, rsum, rc = rs[:, 16:17], rs[:, 17:18], rs[:, 18:19]
            m1, m2, gs, gr = rs[:, 19:20], rs[:, 20:21], rs[:, 21:22], rs[:, 22:23]
            prb, msk, pm, ep, gbuf = (rs[:, 24:32], rs[:, 32:40], rs[:, 40:48],
                                      rs[:, 48:56], rs[:, 56:64])
            tv = self.op(v, "reduce_max", rmx, pxr, axis=AX.X, negate=True,
                         waits=(tm,), inc="V")
            te = self.op(s, "activation", prb, pxr, AF.Exp, bias=rmx,
                         accum_out=rsum, waits=(tv,), inc="S")
            prev_te = te
            t_rc = self.op(v, "reciprocal", rc, rsum, waits=(te,), inc="V")
            self.op(v, "tensor_scalar", prb, prb, rc, None, ALU.mult, fwaits=(t_rc,))
            t_m1 = self.op(v, "reduce_max", m1, prb, axis=AX.X, inc="V")
            self.op(v, "tensor_scalar", msk, prb, m1, None, ALU.is_ge, fwaits=(t_m1,))
            self.op(v, "scalar_tensor_tensor", pm, msk, -1e9, prb, ALU.mult, ALU.add)
            t_m2 = self.op(v, "reduce_max", m2, pm, axis=AX.X, inc="V")
            tv = self.op(v, "tensor_scalar", msk, prb, m2, None, ALU.is_ge,
                         fwaits=(t_m2,), inc="V")
            te = self.op(s, "activation", ep, prb, AF.Exp, waits=(tv,), inc="S")
            self.op(v, "tensor_mul", gbuf, ep, msk, waits=(te,))
            self.op(v, "reduce_sum", gs, gbuf, axis=AX.X)
            t_gr = self.op(v, "reciprocal", gr, gs, inc="V")
            self.op(v, "tensor_scalar", gbuf, gbuf, gr, None, ALU.mult, fwaits=(t_gr,))
            last_router = self.op(v, "tensor_mul", self.comb[:, tile * 8:tile * 8 + 8],
                                  gbuf, self.scale_half[:, :],
                                  waits=_clean(comb_guard) if tile == 0 else (),
                                  inc="V")
        self.setg("px", last_router)
        if not experts:
            self.setg("comb", last_router)
            return
        abank = self.bank_init(4)
        ybank = {0: self.use2("p2", 0), 1: self.use2("p2", 1)}
        w2_guard = self.use("ar1")
        h_guard = self.use("ar2")
        w1_guard = {i: self.use("ar3") for i in range(4)}
        ai = yi = 0
        last_evac = None
        b_last = None
        RING = ("dWa", "dWb", "dWc", "dWd")
        for e in range(8):
            pre_toks = {}
            for fc in range(4):
                slot = fc % 4
                w1dst = self.ar3[:, slot * 1024:(slot + 1) * 1024]
                if first:
                    w1src = d["w1_t"][e, :, fc * 128:(fc + 1) * 128]
                else:
                    w1src = d["w1_bf"][e, :, fc * 128:(fc + 1) * 128]
                pre_toks[fc] = self.w_cast_load(w1dst, w1src, 128,
                                                waits=w1_guard[slot], sem=RING[slot])
            w2toks = []
            for i4 in range(4):
                dst = self.ar1[:, i4 * 8192:(i4 + 1) * 8192]
                if first:
                    srcw = d["w2_t"][e, i4 * 1024:(i4 + 1) * 1024, :]
                    w2toks.append(self.w_cast_load(dst, srcw, 1024, waits=w2_guard))
                else:
                    srcw = d["w2_bf"][e, i4 * 1024:(i4 + 1) * 1024, :]
                    w2toks.append(self.w_cast_load(dst, srcw, 1024, waits=w2_guard))
            w2tok = w2toks[-1]
            if first:
                w2_store = []
                for i4 in range(4):
                    dst3 = d["w2_bf"][e, i4 * 1024:(i4 + 1) * 1024, :].rearrange(
                        "(c p) f -> p c f", p=128)
                    src3 = self.ar1[:, i4 * 8192:(i4 + 1) * 8192].rearrange(
                        "p (c f) -> p c f", f=1024)
                    w2st = self.dma("sync", dst3, src3,
                                    waits=(w2toks[-1],),
                                    sem="dStW2")
                    w2_store.append(w2st)
            else:
                w2_store = []
            for qtr in range(4):
                silu_last = None
                st_all = None
                ai += ai % 2
                for fc in range(32):
                    slot = fc % 4
                    w1dst = self.ar3[:, slot * 1024:(slot + 1) * 1024]
                    if first and qtr == 0:
                        w1src = d["w1_t"][e, :, fc * 128:(fc + 1) * 128]
                    else:
                        w1src = d["w1_bf"][e, :, fc * 128:(fc + 1) * 128]
                    if qtr == 0 and fc < 4:
                        w1tok = pre_toks[fc]
                    else:
                        w1tok = self.w_cast_load(w1dst, w1src, 128,
                                                 waits=w1_guard[slot],
                                                 sem=RING[slot])
                    pb, poff = self.A_banks[ai % 4]
                    psA = pb[:, poff:poff + 512]
                    wts = _clean((w1tok, abank[ai % 4], xnt_ok))
                    tm = None
                    for c in range(NC_D):
                        tm = self.op(t, "matmul", psA,
                                     w1dst[:, c * 128:(c + 1) * 128],
                                     self.xnt[:, c * 2048 + qtr * 512: c * 2048 + (qtr + 1) * 512],
                                     start=(c == 0), stop=(c == NC_D - 1),
                                     waits=wts if c == 0 else (),
                                     inc="T" if c == NC_D - 1 else None)
                    fi = fc
                    if fc % 2 == 1:
                        stok = self.op(s, "activation",
                                       self.ar2[:, (fi - 1) * 512:(fi + 1) * 512],
                                       pb[:, poff - 512:poff + 512], AF.Silu,
                                       waits=_clean([tm] + (h_guard if fi == 1 else [])),
                                       inc="S")
                        silu_last = stok
                        abank[(ai - 1) % 4] = stok
                        abank[ai % 4] = stok
                    ai += 1
                    if first and qtr == 0:
                        dstc = d["w1_bf"][e, :, fc * 128:(fc + 1) * 128]
                        d3 = dstc.rearrange("(c p) f -> p c f", p=128)
                        s3 = w1dst.rearrange("p (c f) -> p c f", f=128)
                        sttok = self.dma("sync", d3, s3, waits=(w1tok,),
                                         sem="dSt" + str(slot))
                        w1_guard[slot] = [tm, sttok]
                    else:
                        w1_guard[slot] = [tm]
                for tt in range(4):
                    gt = qtr * 4 + tt
                    for dh in range(2):
                        psY = self.p2[:, (yi % 2) * 512:(yi % 2) * 512 + 512]
                        wts = _clean((silu_last, w2tok, ybank[yi % 2]))
                        tm = None
                        for fi in range(32):
                            tm = self.op(t, "matmul", psY,
                                         self.ar2[:, fi * 512 + tt * 128: fi * 512 + tt * 128 + 128],
                                         self.ar1[:, fi * 1024 + dh * 512: fi * 1024 + dh * 512 + 512],
                                         start=(fi == 0), stop=(fi == 31),
                                         waits=wts if fi == 0 else (),
                                         inc="T" if fi == 31 else None)
                        b_last = tm
                        xsl = self.xr[:, gt * 1024 + dh * 512: gt * 1024 + dh * 512 + 512]
                        last_evac = self.op(v, "scalar_tensor_tensor", xsl, psY,
                                            self.comb[:, gt * 8 + e: gt * 8 + e + 1],
                                            xsl, ALU.mult, ALU.add,
                                            waits=_clean((tm,)),
                                            fwaits=(last_router,), inc="V")
                        ybank[yi % 2] = last_evac
                        yi += 1
                h_guard = [b_last]
            w2_guard = [b_last] + ([self.dma_done("dStW2")] if w2_store else [])
        self.setg("ar1", b_last,
                  self.dma_done("dStW2") if first else None)
        self.setg("ar2", b_last)
        self.setg("ar3", *(w1_guard[0] + w1_guard[1] + w1_guard[2] + w1_guard[3]))
        self.setg("xr", last_evac)
        self.setg("comb", last_evac)
        self.setg("p0p1", *[abank[i] for i in range(4)])
        self.set2("p2", 0, ybank[0])
        self.set2("p2", 1, ybank[1])

    def emit_attn(self, d, layer):
        t, v, s = "tensor", "vector", "scalar"
        is_l0 = (layer == "A0")
        xnt_ok = self.emit_ln()
        QT0, KT0, V0, CT0, WO0 = 0, 8192, 16384, 24704, 26752
        ET = [self.ar2[:, 0:8192], self.ar2[:, 8192:16384]]
        OT0 = 2048
        self.sbank = self.bank_init(4)
        self.bi = 0
        ar1_guard = self.use("ar1")
        ar2_guard = self.use("ar2")
        ar3_guard = self.use("ar3")
        px_readers = _clean(self.use("px"))
        ctx_tok = None
        if is_l0:
            wetok = self.w_cast_load(self.ar3[:, 0:1024], d["we_t"], 128,
                                     waits=ar3_guard, sem="dWa")
            ar3_guard = []
            qkg = {0: [wetok], 1: []}
            for qc in range(4):
                pb, poff = self.A_banks[self.bi % 4]
                psc = pb[:, poff:poff + 512]
                wts = _clean([wetok, self.sbank[self.bi % 4], xnt_ok])
                tm = None
                for c in range(NC_D):
                    tm = self.op(t, "matmul", psc,
                                 self.ar3[:, c * 128:(c + 1) * 128],
                                 self.xnt[:, c * 2048 + qc * 512: c * 2048 + (qc + 1) * 512],
                                 start=(c == 0), stop=(c == NC_D - 1),
                                 waits=wts if c == 0 else (),
                                 inc="T" if c == NC_D - 1 else None)
                ctx_tok = self.op(s, "activation",
                                  self.ar1[:, CT0 + qc * 512: CT0 + (qc + 1) * 512],
                                  psc, AF.Copy, scale=SQ01,
                                  waits=_clean([tm] + (ar1_guard if qc == 0 else [])),
                                  inc="S")
                self.sbank[self.bi % 4] = ctx_tok
                self.bi += 1
            qk_stream_guard = {0: [ctx_tok], 1: []}
        else:
            qk_stream_guard = {0: ar3_guard, 1: ar3_guard}
        last_wo_mm = None
        xr_guard = self.use("xr")
        last_xr_add = xr_guard[-1] if xr_guard else None
        ot_guard = self.use("ar3")
        pv_last = {}
        for grp in range(2):
            wv_dst = self.ar2[:, 0:4096]
            if is_l0:
                wv_src = d["wv_t"][:, grp * 512:(grp + 1) * 512]
            else:
                wv_src = d["qkv_w"][:, 2048 + grp * 512: 2048 + (grp + 1) * 512]
            wvtok = self.w_cast_load(wv_dst, wv_src, 512, waits=ar2_guard, sem="dWv")
            qkvb = self.ar2[0:1, 8192:11264]
            qbtok = None
            if not is_l0:
                qbtok = self.dma("gpsimd", qkvb, d["qkv_b"][:, :], waits=ar2_guard, sem="dQB")
            qkv_last = None
            for which, base, scale in (("q", QT0, 0.125), ("k", KT0, 1.0)):
                for j in range(4):
                    mcol = grp * 512 + j * 128
                    par = j % 2
                    wdst = self.ar3[:, par * 1024:(par + 1) * 1024]
                    if is_l0:
                        wsrc = d["wq_t" if which == "q" else "wk_t"][:, mcol:mcol + 128]
                    else:
                        wofs = 0 if which == "q" else 1024
                        wsrc = d["qkv_w"][:, wofs + mcol: wofs + mcol + 128]
                    wtok = self.w_cast_load(wdst, wsrc, 128,
                                            waits=qk_stream_guard[par],
                                            sem=("dWa" if par == 0 else "dWb"))
                    tm = None
                    for nch in range(4):
                        pb, poff = self.A_banks[self.bi % 4]
                        psq = pb[:, poff:poff + 512]
                        wts = _clean((wtok, self.sbank[self.bi % 4], xnt_ok))
                        for c in range(NC_D):
                            last_of_group = (c == NC_D - 1) and is_l0
                            tm = self.op(t, "matmul", psq,
                                         wdst[:, c * 128:(c + 1) * 128],
                                         self.xnt[:, c * 2048 + nch * 512: c * 2048 + (nch + 1) * 512],
                                         start=(c == 0), stop=last_of_group,
                                         waits=wts if c == 0 else (),
                                         inc="T" if last_of_group else None)
                        if not is_l0:
                            bofs = (0 if which == "q" else 1024) + mcol
                            tm = self.op(t, "matmul", psq,
                                         qkvb[0:1, bofs:bofs + 128],
                                         self.onesb[0:1, 0:512],
                                         start=False, stop=True,
                                         waits=self.t_consts + (qbtok,), inc="T")
                        etok = self.op(s, "activation",
                                       self.ar1[:, base + j * 2048 + nch * 512:
                                                base + j * 2048 + nch * 512 + 512],
                                       psq, AF.Copy, scale=scale,
                                       waits=_clean([tm] + (ar1_guard if self.bi < 4 else [])),
                                       inc="S")
                        qkv_last = etok
                        self.sbank[self.bi % 4] = etok
                        self.bi += 1
                    qk_stream_guard[par] = [qkv_last]
            pbtok = None
            if (not is_l0) and grp == 1:
                pbtok = self.dma("gpsimd", self.ar3[0:1, 0:1024], d["proj_b"][:, :],
                                 waits=_clean(qk_stream_guard[0]), sem="dPB")
            v_last = None
            vmm_last = None
            for tt in range(NT):
                half = tt % 2
                psv = self.p2[:, half * 512:half * 512 + 512]
                wts = _clean((wvtok, self.use2("p2", half), xnt_ok))
                tm = None
                for c in range(NC_D):
                    stop = (c == NC_D - 1) and is_l0
                    tm = self.op(t, "matmul", psv,
                                 self.xnt[:, c * 2048 + tt * 128: c * 2048 + tt * 128 + 128],
                                 wv_dst[:, c * 512:(c + 1) * 512],
                                 start=(c == 0), stop=stop,
                                 waits=wts if c == 0 else (),
                                 inc="T" if stop else None)
                if not is_l0:
                    tm = self.op(t, "matmul", psv,
                                 self.onesb[0:1, 0:128],
                                 qkvb[0:1, 2048 + grp * 512: 2048 + (grp + 1) * 512],
                                 start=False, stop=True, waits=(qbtok,), inc="T")
                vmm_last = tm
                dst = _ap3(self.ar1, V0 + tt * 520, [[32768, 128], [65, 8], [1, 64]])
                src = psv.rearrange("p (h f) -> p h f", f=64)
                vtok = self.op(v, "tensor_copy", dst, src,
                               waits=_clean([tm] + (ar1_guard if tt == 0 else [])),
                               inc="V")
                onescol = _ap3(self.ar1, V0 + tt * 520 + 64, [[32768, 128], [65, 8], [1, 1]])
                v_last = self.op(v, "memset", onescol, 1.0, inc="V")
                self.set2("p2", half, v_last)
            wo_src = (d["wo_t"] if is_l0 else d["proj_w"])[grp * 512:(grp + 1) * 512, :]
            wotok = self.w_cast_load(self.ar1[:, WO0:WO0 + 4096], wo_src, 1024,
                                     waits=_clean([last_wo_mm] +
                                                  (ar1_guard if grp == 0 else [])),
                                     sem="dWo")
            et_guard = {0: ar2_guard + [vmm_last], 1: ar2_guard + [vmm_last]}
            ar2_guard = []
            for qb in range(4):
                mult_toks = {}
                exp_done = {}

                def emit_S(l, qb=qb):
                    nonlocal et_guard
                    par = l % 2
                    prow = (l % 2) * 64
                    jt = l // 2
                    exp_last = None
                    self.bi += self.bi % 2
                    for kt in range(NT):
                        pb, poff = self.A_banks[self.bi % 4]
                        pss = pb[:, poff:poff + 512]
                        wts = _clean([self.sbank[self.bi % 4], qkv_last, v_last,
                                      ctx_tok if is_l0 else None])
                        if is_l0:
                            self.op(t, "matmul", pss,
                                    self.ar1[prow:prow + 64,
                                             KT0 + jt * 2048 + kt * 128: KT0 + jt * 2048 + kt * 128 + 128],
                                    self.ar1[prow:prow + 64,
                                             QT0 + jt * 2048 + qb * 512: QT0 + jt * 2048 + (qb + 1) * 512],
                                    start=True, stop=False,
                                    tile_position=(prow, 0),
                                    waits=wts)
                            tm = self.op(t, "matmul", pss,
                                         self.ar1[:, CT0 + kt * 128: CT0 + kt * 128 + 128],
                                         self.ar1[:, CT0 + qb * 512: CT0 + (qb + 1) * 512],
                                         start=False, stop=True, inc="T")
                        else:
                            tm = self.op(t, "matmul", pss,
                                         self.ar1[prow:prow + 64,
                                                  KT0 + jt * 2048 + kt * 128: KT0 + jt * 2048 + kt * 128 + 128],
                                         self.ar1[prow:prow + 64,
                                                  QT0 + jt * 2048 + qb * 512: QT0 + jt * 2048 + (qb + 1) * 512],
                                         start=True, stop=True,
                                         tile_position=(prow, 0),
                                         waits=wts, inc="T")
                        if kt % 2 == 1:
                            pairbase = (self.bi - 1) % 4
                            tmi = self.op(s, "activation",
                                          ET[par][:, (kt - 1) * 512:(kt + 1) * 512],
                                          pb[:, poff - 512:poff + 512], AF.Exp,
                                          bias=(self.bias8[:, :] if is_l0 else 0.0),
                                          waits=_clean([tm, pv_last.get(par),
                                                        self.bias8_tok if is_l0 else None] +
                                                       (et_guard[par] if kt == 1 else [])),
                                          inc="S")
                            exp_last = tmi
                            self.sbank[pairbase] = tmi
                            self.sbank[self.bi % 4] = tmi
                        self.bi += 1
                    et_guard[par] = []
                    exp_done[l] = exp_last

                def emit_PV(l, qb=qb):
                    nonlocal ot_guard
                    par = l % 2
                    prow = (l % 2) * 64
                    jt = l // 2
                    wts = _clean([exp_done[l], v_last] + list(self.use("px")))
                    tmm = None
                    for kt in range(NT):
                        tmm = self.op(t, "matmul", self.px[0:65, 0:512],
                                      _ap3(self.ar1, V0 + kt * 520 + l * 65,
                                           [[32768, 128], [1, 65]]),
                                      ET[par][:, kt * 512:(kt + 1) * 512],
                                      start=(kt == 0), stop=(kt == NT - 1),
                                      waits=wts if kt == 0 else (),
                                      inc="T" if kt == NT - 1 else None)
                    pv_last[par] = tmm
                    ot_sl = self.ar3[prow:prow + 64,
                                     OT0 + jt * 512: OT0 + jt * 512 + 512]
                    ucp = self.op(s, "activation", ot_sl, self.px[0:64, 0:512],
                                  AF.Copy,
                                  waits=_clean([tmm] + (ot_guard if l < 2 else [])),
                                  inc="S")
                    rcp = self.op(v, "tensor_copy", self.rrow[:, :], self.px[64:65, 0:512],
                                  waits=(tmm,), inc="V")
                    rrc = self.op(v, "reciprocal", self.rrow[:, :], self.rrow[:, :],
                                  inc="V")
                    bc = self.op(t, "matmul", self.px[64:128, 0:512],
                                 self.onesf[0:1, 0:64], self.rrow[0:1, 0:512],
                                 start=True, stop=True,
                                 waits=_clean((rrc, ucp)), inc="T")
                    mt = self.op(v, "tensor_mul", ot_sl, ot_sl,
                                 self.px[64:128, 0:512],
                                 waits=(bc,), inc="V")
                    mult_toks[l] = mt
                    self.setg("px", mt, ucp, rcp)

                emit_S(0)
                for l in range(8):
                    if l < 7:
                        emit_S(l + 1)
                    emit_PV(l)
                ot_guard = []
                wo_toks = []
                for qt in range(4):
                    gt = qb * 4 + qt
                    for dh in range(2):
                        psr = self.p2[:, dh * 512:(dh + 1) * 512]
                        wts = _clean((mult_toks[7], wotok, self.use2("p2", dh)))
                        tm = None
                        add_bias = (not is_l0) and grp == 1
                        for j in range(4):
                            stop = (j == 3) and not add_bias
                            tm = self.op(t, "matmul", psr,
                                         self.ar3[:, OT0 + j * 512 + qt * 128:
                                                  OT0 + j * 512 + qt * 128 + 128],
                                         self.ar1[:, WO0 + j * 1024 + dh * 512:
                                                  WO0 + j * 1024 + dh * 512 + 512],
                                         start=(j == 0), stop=stop,
                                         waits=wts if j == 0 else (),
                                         inc="T" if stop else None)
                        if add_bias:
                            tm = self.op(t, "matmul", psr,
                                         self.onesb[0:1, 0:128],
                                         self.ar3[0:1, dh * 512:(dh + 1) * 512],
                                         start=False, stop=True,
                                         waits=_clean((pbtok,)), inc="T")
                        wo_toks.append(tm)
                        xsl = self.xr[:, gt * 1024 + dh * 512: gt * 1024 + dh * 512 + 512]
                        last_xr_add = self.op(v, "tensor_add", xsl, xsl, psr,
                                              waits=_clean((tm, last_xr_add)), inc="V")
                        self.set2("p2", dh, last_xr_add)
                last_wo_mm = wo_toks[-1]
                ot_guard = [last_wo_mm]
            ar2_guard = list(_clean([pv_last.get(0), pv_last.get(1)]))
        self.setg("ar1", last_wo_mm, pv_last.get(0), pv_last.get(1))
        self.setg("ar2", pv_last.get(0), pv_last.get(1))
        self.setg("ar3", last_wo_mm)
        self.setg("xr", last_xr_add)
        self.setg("p0p1", *[self.sbank[i] for i in range(4)])

    def emit_f1(self, d, expert_set=tuple(range(9)), first=True):
        t, v, s = "tensor", "vector", "scalar"
        xnt_ok = self.emit_ln()

        prev_te = None
        pxg = self.use("px")
        comb_guard = self.use("comb")
        last_gate = None
        for tile in range(NT):
            off = (tile % 4) * 128
            pxr = self.px[:, off:off + 8]
            wts = _clean([xnt_ok, prev_te] + (list(pxg) if tile == 0 else []) +
                         list(self.t_consts))
            tm = None
            for c in range(NC_D):
                tm = self.op(t, "matmul", pxr,
                             self.xnt[:, c * 2048 + tile * 128: c * 2048 + tile * 128 + 128],
                             self.gateT[:, c * 8:(c + 1) * 8],
                             start=(c == 0), stop=(c == NC_D - 1),
                             waits=wts if c == 0 else (),
                             inc="T" if c == NC_D - 1 else None)
            rs = self.rsc
            sco, bia, msk, pm, gsel = (rs[:, 24:32], rs[:, 32:40], rs[:, 40:48],
                                       rs[:, 48:56], rs[:, 56:64])
            m1, m2 = rs[:, 16:17], rs[:, 17:18]
            te = self.op(s, "activation", sco, pxr, AF.Sigmoid,
                         waits=_clean((tm, last_gate)), inc="S")
            prev_te = te
            self.op(v, "tensor_add", bia, sco, self.biasb[:, :], waits=(te,))
            t_m1 = self.op(v, "reduce_max", m1, bia, axis=AX.X, inc="V")
            self.op(v, "tensor_scalar", msk, bia, m1, None, ALU.is_ge, fwaits=(t_m1,))
            self.op(v, "scalar_tensor_tensor", pm, msk, -1e9, bia, ALU.mult, ALU.add)
            t_m2 = self.op(v, "reduce_max", m2, pm, axis=AX.X, inc="V")
            self.op(v, "tensor_scalar", msk, bia, m2, None, ALU.is_ge, fwaits=(t_m2,))
            self.op(v, "tensor_mul", gsel, msk, sco)
            last_gate = self.op(v, "scalar_tensor_tensor",
                                self.comb[:, tile * 8:tile * 8 + 8],
                                gsel, 0.5, gsel, ALU.mult, ALU.mult,
                                waits=_clean(comb_guard) if tile == 0 else (),
                                inc="V")
        self.setg("px", last_gate)
        abank = self.bank_init(4)
        ybank = {0: self.use2("p2", 0), 1: self.use2("p2", 1)}
        wd_guard = self.use("ar2")
        h_guard = self.use("ar1")
        wu_guard = {i: self.use("ar3") for i in range(4)}
        ai = yi = 0
        last_evac = None
        b_last = None
        RING = ("dWa", "dWb", "dWc", "dWd")

        def wu_src(e, fc):
            if first:
                return (d["wu_t"][e, :, fc * 128:(fc + 1) * 128] if e < 8
                        else d["swu_t"][:, fc * 128:(fc + 1) * 128])
            return (d["wu_bf"][e, :, fc * 128:(fc + 1) * 128] if e < 8
                    else d["swu_bf"][:, fc * 128:(fc + 1) * 128])

        for e in expert_set:
            pre_toks = {}
            for fc in range(4):
                par = fc % 4
                wudst = self.ar3[:, par * 1024:(par + 1) * 1024]
                pre_toks[fc] = self.w_cast_load(wudst, wu_src(e, fc), 128,
                                                waits=wu_guard[par], sem=RING[par])
            wdsrc = (d["wd_t"][e] if e < 8 else d["swd_t"]) if first else                     (d["wd_bf"][e] if e < 8 else d["swd_bf"])
            wdtoks = []
            for (r0, r1) in ((0, 6), (6, 11)):
                dst = self.ar2[:, r0 * 1024: r1 * 1024]
                src = wdsrc[r0 * 128: r1 * 128, :]
                wdtoks.append(self.w_cast_load(dst, src, 1024, waits=wd_guard))
            wdtok = wdtoks[-1]
            wd_store = []
            if first:
                wdbf = d["wd_bf"][e] if e < 8 else d["swd_bf"]
                for (r0, r1) in ((0, 6), (6, 11)):
                    dst3 = wdbf[r0 * 128: r1 * 128, :].rearrange("(c p) f -> p c f", p=128)
                    src3 = self.ar2[:, r0 * 1024: r1 * 1024].rearrange(
                        "p (c f) -> p c f", f=1024)
                    wd_store.append(self.dma("sync", dst3, src3,
                                             waits=(wdtok,), sem="dStW2"))
            sq_last = None
            for fc in range(11):
                par = fc % 4
                wudst = self.ar3[:, par * 1024:(par + 1) * 1024]
                if fc < 4:
                    wutok = pre_toks[fc]
                else:
                    wutok = self.w_cast_load(wudst, wu_src(e, fc), 128,
                                             waits=wu_guard[par], sem=RING[par])
                ai += ai % 2
                for tc in range(4):
                    pb, poff = self.A_banks[ai % 4]
                    psA = pb[:, poff:poff + 512]
                    wts = _clean((wutok, abank[ai % 4], xnt_ok))
                    tm = None
                    for c in range(NC_D):
                        tm = self.op(t, "matmul", psA,
                                     wudst[:, c * 128:(c + 1) * 128],
                                     self.xnt[:, c * 2048 + tc * 512: c * 2048 + (tc + 1) * 512],
                                     start=(c == 0), stop=(c == NC_D - 1),
                                     waits=wts if c == 0 else (),
                                     inc="T" if c == NC_D - 1 else None)
                    if tc % 2 == 1:
                        hsl = self.ar1[:, fc * 2048 + (tc - 1) * 512: fc * 2048 + (tc + 1) * 512]
                        rtok = self.op(s, "activation", hsl,
                                       pb[:, poff - 512:poff + 512], AF.Relu,
                                       waits=_clean([tm] + (h_guard if fc == 0 and tc == 1 else [])),
                                       inc="S")
                        sq_last = self.op(v, "tensor_mul", hsl, hsl, hsl,
                                          waits=(rtok,), inc="V")
                        abank[(ai - 1) % 4] = sq_last
                        abank[ai % 4] = sq_last
                    ai += 1
                if first:
                    wubf = d["wu_bf"][e, :, fc * 128:(fc + 1) * 128] if e < 8 \
                        else d["swu_bf"][:, fc * 128:(fc + 1) * 128]
                    d3 = wubf.rearrange("(c p) f -> p c f", p=128)
                    s3 = wudst.rearrange("p (c f) -> p c f", f=128)
                    sttok = self.dma("sync", d3, s3, waits=(wutok,),
                                     sem="dSt" + str(par))
                    wu_guard[par] = [tm, sttok]
                else:
                    wu_guard[par] = [tm]
            for tt in range(NT):
                for dh in range(2):
                    psY = self.p2[:, (yi % 2) * 512:(yi % 2) * 512 + 512]
                    wts = _clean((sq_last, wdtok, ybank[yi % 2]))
                    tm = None
                    for fc in range(11):
                        tm = self.op(t, "matmul", psY,
                                     self.ar1[:, fc * 2048 + tt * 128: fc * 2048 + tt * 128 + 128],
                                     self.ar2[:, fc * 1024 + dh * 512: fc * 1024 + dh * 512 + 512],
                                     start=(fc == 0), stop=(fc == 10),
                                     waits=wts if fc == 0 else (),
                                     inc="T" if fc == 10 else None)
                    b_last = tm
                    xsl = self.xr[:, tt * 1024 + dh * 512: tt * 1024 + dh * 512 + 512]
                    scl = (self.comb[:, tt * 8 + e: tt * 8 + e + 1] if e < 8 else 0.5)
                    last_evac = self.op(v, "scalar_tensor_tensor", xsl, psY,
                                        scl, xsl, ALU.mult, ALU.add,
                                        waits=_clean((tm,)),
                                        fwaits=(last_gate,) if e < 8 else (),
                                        inc="V")
                    ybank[yi % 2] = last_evac
                    yi += 1
            h_guard = [b_last]
            wd_guard = [b_last] + ([self.dma_done("dStW2")] if wd_store else [])
        self.setg("ar1", b_last)
        self.setg("ar2", b_last,
                  self.dma_done("dStW2") if first else None)
        self.setg("ar3", *(wu_guard[0] + wu_guard[1] + wu_guard[2] + wu_guard[3]))
        self.setg("xr", last_evac)
        self.setg("comb", last_evac)
        self.setg("p0p1", *[abank[i] for i in range(4)])
        self.set2("p2", 0, ybank[0])
        self.set2("p2", 1, ybank[1])

    def barrier(self):
        toks = []
        for eng, sem in (("tensor", "T"), ("vector", "V"), ("scalar", "S")):
            toks.append(self.op(eng, "nop", inc=sem))
        for s in self.DMA_SEMS:
            toks.append(self.dma_done(s))
        for eng in ("tensor", "vector", "scalar", "gpsimd", "sync"):
            for tk in toks:
                self.wait_force(eng, tk)

    def emit_epilogue(self, d):
        xr_done = _clean(self.use("xr"))
        xo3 = d["xout"].rearrange("(c p) f -> p c f", p=128)
        xr3 = self.xr[:, :].rearrange("p (c f) -> p c f", f=1024)
        for i in range(4):
            self.dma("sync", xo3[:, i * 4:(i + 1) * 4, :],
                     xr3[:, i * 4:(i + 1) * 4, :], waits=xr_done)
        self.wait("sync", self.dma_done("dX"))

    def build(self, d):
        self.emit_prologue(d, "F0" in self.layers, "F1" in self.layers)
        f0_seen = 0
        f1_seen = 0
        for l in self.layers:
            if l == "F0":
                self.emit_f0(d, first=(f0_seen == 0))
                f0_seen += 1
            elif l == "A0":
                self.emit_attn(d, "A0")
            elif l == "A1":
                self.emit_attn(d, "A1")
            elif l == "F1":
                self.emit_f1(d, first=(f1_seen == 0))
                f1_seen += 1
        self.emit_epilogue(d)
        with self.nc.Block() as blk:
            for name, reg in (("tensor", blk.tensor), ("vector", blk.vector),
                              ("scalar", blk.scalar), ("gpsimd", blk.gpsimd),
                              ("sync", blk.sync)):
                ops = self.prog[name]

                def body(e, ops=ops):
                    for f in ops:
                        f(e)
                reg(body)


def _declare_inputs(nc, layers):
    d = {}

    def dt(name, shape, dtype=F32, kind="ExternalInput"):
        d[name] = nc.dram_tensor(name, shape, dtype, kind=kind).ap()
    dt("xin", [2048, 1024])
    dt("xout", [2048, 1024], kind="ExternalOutput")
    dt("ident", [128, 128], BF16)
    dt("ones_bf", [1, 512], BF16)
    dt("ones_f", [1, 128], F32)
    if "F0" in layers:
        dt("rw_t", [1024, 8])
        dt("esc_row", [1, 8])
        dt("w1_t", [8, 1024, 4096])
        dt("w2_t", [8, 4096, 1024])
        dt("w1_bf", [8, 1024, 4096], BF16, kind="Internal")
        dt("w2_bf", [8, 4096, 1024], BF16, kind="Internal")
    if "A0" in layers:
        dt("wq_t", [1024, 1024])
        dt("wk_t", [1024, 1024])
        dt("wv_t", [1024, 1024])
        dt("wo_t", [1024, 1024])
        dt("we_t", [1024, 128])
    if "A1" in layers:
        dt("qkv_w", [1024, 3072])
        dt("proj_w", [1024, 1024])
    if "A1" in layers:
        dt("qkv_b", [1, 3072])
        dt("proj_b", [1, 1024])
    if "F1" in layers:
        dt("gate_t", [1024, 8])
        dt("eb_row", [1, 8])
        dt("wu_t", [8, 1024, 1408])
        dt("wd_t", [8, 1408, 1024])
        dt("swu_t", [1024, 1408])
        dt("swd_t", [1408, 1024])
        dt("wu_bf", [8, 1024, 1408], BF16, kind="Internal")
        dt("wd_bf", [8, 1408, 1024], BF16, kind="Internal")
        dt("swu_bf", [1024, 1408], BF16, kind="Internal")
        dt("swd_bf", [1408, 1024], BF16, kind="Internal")
    return d


def build_nc(layers, tag=""):
    nc = bass.Bass("TRN2", target_bir_lowering=False, debug=False,
                   dynamic_dma_scratch_size=16384)
    d = _declare_inputs(nc, layers)
    mk = MK(nc, layers, tag=tag)
    ctx = ExitStack()
    with ctx:
        mk.alloc(ctx, has_l1=("A1" in layers or "F1" in layers))
        mk.build(d)
    return nc


def _host_weights(inputs):
    c = np.ascontiguousarray
    w = {}
    w["rw_t"] = c(inputs["L0_ffn_router_w"].T)
    w["esc_row"] = c(inputs["L0_ffn_expert_scale"].reshape(1, 8))
    w["w1_t"] = c(inputs["L0_ffn_w1"].transpose(0, 2, 1))
    w["w2_t"] = c(inputs["L0_ffn_w2"].transpose(0, 2, 1))
    w["wq_t"] = c(inputs["L0_attn_Wq"].T)
    w["wk_t"] = c(inputs["L0_attn_Wk"].T)
    w["wv_t"] = c(inputs["L0_attn_Wv"].T)
    w["wo_t"] = c(inputs["L0_attn_Wo"].T)
    w["we_t"] = c(inputs["L0_attn_We"].T)
    w["qkv_w"] = c(inputs["L1_attn_qkv_w"])
    w["qkv_b"] = c(inputs["L1_attn_qkv_b"].reshape(1, 3072))
    w["proj_w"] = c(inputs["L1_attn_proj_w"])
    w["proj_b"] = c(inputs["L1_attn_proj_b"].reshape(1, 1024))
    w["gate_t"] = c(inputs["L1_ffn_gate_w"].T)
    w["eb_row"] = c(inputs["L1_ffn_expert_bias"].reshape(1, 8))
    w["wu_t"] = c(inputs["L1_ffn_w_up"].transpose(0, 2, 1))
    w["wd_t"] = c(inputs["L1_ffn_w_down"].transpose(0, 2, 1))
    w["swu_t"] = c(inputs["L1_ffn_shared_w_up"].T)
    w["swd_t"] = c(inputs["L1_ffn_shared_w_down"].T)
    import ml_dtypes
    w["ident"] = np.eye(128, dtype=ml_dtypes.bfloat16)
    w["ones_bf"] = np.ones((1, 512), dtype=ml_dtypes.bfloat16)
    w["ones_f"] = np.ones((1, 128), dtype=np.float32)
    return w


def _needed_names(layers):
    names = ["ident", "ones_bf", "ones_f"]
    if "F0" in layers:
        names += ["rw_t", "esc_row", "w1_t", "w2_t"]
    if "A0" in layers:
        names += ["wq_t", "wk_t", "wv_t", "wo_t", "we_t"]
    if "A1" in layers:
        names += ["qkv_w", "proj_w", "qkv_b", "proj_b"]
    if "F1" in layers:
        names += ["gate_t", "eb_row", "wu_t", "wd_t", "swu_t", "swd_t"]
    return names


def run_layers(x, inputs, layers, n_cores=8):
    w = _host_weights(inputs)
    nc = build_nc(layers)
    need = _needed_names(layers)
    in_maps = []
    for cidx in range(n_cores):
        m = {"xin": np.ascontiguousarray(x[cidx])}
        for k in need:
            m[k] = w[k]
        in_maps.append(m)
    res = run_bass_kernel_spmd(nc, in_maps, core_ids=list(range(n_cores)))
    return np.stack([np.asarray(r["xout"]) for r in res.results], axis=0)


def kernel(**inputs):
    x = np.ascontiguousarray(np.asarray(inputs["L0_attn_X"], dtype=np.float32))
    out = run_layers(x, inputs, list(ALL_LAYERS), n_cores=8)
    return out.astype(np.float32)


# version 11
# speedup vs baseline: 1.0964x; 1.0964x over previous
import numpy as np
from contextlib import ExitStack

import concourse.bass as bass
import concourse.mybir as mybir
from concourse.bass_utils import run_bass_kernel_spmd

F32 = mybir.dt.float32
BF16 = mybir.dt.bfloat16
AF = mybir.ActivationFunctionType
ALU = mybir.AluOpType
AX = mybir.AxisListType

NT = 16
NC_D = 8
SQ01 = float(np.sqrt(0.1))

ALL_LAYERS = ("F0", "A0", "F0", "F1", "A1", "F1")


def _ap3(t, off, dims):
    return bass.AP(t, off, dims)


def _clean(toks):
    return tuple(x for x in toks if x is not None)


class MK:
    DMA_SEMS = ("dX", "dW2", "dWa", "dWb", "dWc", "dWd", "dWv", "dWo", "dQB", "dPB", "dCn", "dSt0", "dSt1", "dSt2", "dSt3", "dStW2", "dW2h")

    def __init__(self, nc, layers, tag=""):
        self.nc = nc
        self.layers = layers
        self.tag = tag
        self.prog = {e: [] for e in ("tensor", "vector", "scalar", "gpsimd", "sync")}
        self.cnt = {}
        self.semh = {}
        self.hiwater = {e: {} for e in self.prog}
        self.guards = {}

    def sem(self, name, ctx):
        self.semh[name] = ctx.enter_context(self.nc.semaphore(self.tag + name))
        self.cnt[name] = 0

    def raw(self, eng, fn):
        self.prog[eng].append(fn)

    def wait(self, eng, tok):
        if tok is None:
            return
        sname, val, producer = tok
        if producer == eng:
            return
        if self.hiwater[eng].get(sname, 0) >= val:
            return
        self.hiwater[eng][sname] = val
        h = self.semh[sname]
        self.raw(eng, lambda e, h=h, v=val: e.wait_ge(h, v))

    def wait_force(self, eng, tok):
        if tok is None:
            return
        sname, val, producer = tok
        if self.hiwater[eng].get(sname, 0) >= val:
            return
        self.hiwater[eng][sname] = val
        h = self.semh[sname]
        self.raw(eng, lambda e, h=h, v=val: e.wait_ge(h, v))

    AUTO_CHAIN = {"vector": "V", "scalar": "S"}

    def op(self, eng, method, *args, waits=(), fwaits=(), inc=None, **kw):
        auto = self.AUTO_CHAIN.get(eng)
        if auto is not None and method != "dma_start":
            self.wait_force(eng, (auto, self.cnt[auto], None))
        for t in waits:
            self.wait(eng, t)
        for t in fwaits:
            self.wait_force(eng, t)
        inc_list = []
        tok = None
        if inc is not None:
            name, k = (inc, 1) if isinstance(inc, str) else inc
            self.cnt[name] += k
            prod = None if name in self.DMA_SEMS else eng
            tok = (name, self.cnt[name], prod)
            inc_list = [(self.semh[name], k)]
        if auto is not None and method != "dma_start":
            if inc is None:
                self.cnt[auto] += 1
                tok = (auto, self.cnt[auto], eng)
                inc_list = [(self.semh[auto], 1)]
            else:
                assert inc == auto or (isinstance(inc, tuple) and inc[0] == auto), \
                    f"{eng} op with inc {inc}"

        def cl(e, method=method, args=args, kw=kw, inc_list=inc_list):
            ins = getattr(e, method)(*args, **kw)
            for (h, k) in inc_list:
                ins.then_inc(h, k)
        self.raw(eng, cl)
        return tok

    def dma(self, queue, out, in_, waits=(), sem="dX"):
        return self.op(queue, "dma_start", out, in_, waits=waits, inc=(sem, 16))

    def use(self, buf):
        return list(self.guards.get(buf, ()))

    def setg(self, buf, *toks):
        self.guards[buf] = [t for t in toks if t is not None]

    def use2(self, name, idx):
        return self.guards.get((name, idx), None)

    def set2(self, name, idx, tok):
        self.guards[(name, idx)] = tok

    def bank_init(self, n):
        prev = self.use("p0p1")
        return {i: (prev[i] if i < len(prev) else (prev[-1] if prev else None))
                for i in range(n)}

    def alloc(self, ctx, has_l1=True):
        nc = self.nc
        sb = lambda n, s, d: ctx.enter_context(nc.sbuf_tensor(self.tag + "sb_" + n, s, d))
        ps = lambda n, s, d: ctx.enter_context(nc.psum_tensor(self.tag + n, s, d))
        self.xr = sb("xr", [128, NT * 1024], F32)
        self.xnt = sb("xnt", [128, NC_D * 2048], BF16)
        self.ar1 = sb("ar1", [128, 32768], BF16)
        self.ar2 = sb("ar2", [128, 16384], BF16)
        self.ar3 = sb("ar3", [128, 4096], BF16)
        self.ident = sb("ident", [128, 128], BF16)
        self.onesb = sb("onesb", [1, 512], BF16)
        self.onesf = sb("onesf", [1, 128], F32)
        self.rwT = sb("rwT", [128, 64], BF16)
        self.gateT = sb("gateT", [128, 64], BF16)
        self.esc_row = sb("esc_row", [1, 8], F32)
        self.eb_row = sb("eb_row", [1, 8], F32)
        self.scale_half = sb("scale_half", [128, 8], F32)
        self.biasb = sb("biasb", [128, 8], F32)
        self.comb = sb("comb", [128, NT * 8], F32)
        self.ln_s1 = sb("ln_s1", [128, 16], F32)
        self.ln_ssq = sb("ln_ssq", [128, 16], F32)
        self.ln_m = sb("ln_m", [128, 16], F32)
        self.ln_t = sb("ln_t", [128, 16], F32)
        self.ln_lt = sb("ln_lt", [128, 16], F32)
        self.ln_rstd = sb("ln_rstd", [128, 16], F32)
        self.rsc = sb("rsc", [128, 64], F32)
        self.xn_scr = sb("xn_scr", [128, 1024], BF16)
        self.rrow = sb("rrow", [1, 512], F32)
        self.bias8 = sb("bias8", [128, 1], F32)

        self.p0 = ps("p0", [128, 1024], F32)
        self.p1 = ps("p1", [128, 1024], F32)
        self.p2 = ps("p2", [128, 1024], F32)
        self.px = ps("px", [128, 512], F32)
        self.pt = ps("pt", [128, 1024], BF16)
        self.A_banks = [(self.p0, 0), (self.p0, 512), (self.p1, 0), (self.p1, 512)]
        for s in self.DMA_SEMS + ("T", "S", "V"):
            self.sem(s, ctx)

    def w_cast_load(self, dst2d, src2d, cols_per_chunk, waits=(), sem="dW2",
                    queue="gpsimd"):
        dst3 = dst2d.rearrange("p (c f) -> p c f", f=cols_per_chunk)
        src3 = src2d.rearrange("(c p) f -> p c f", p=128)
        return self.dma(queue, dst3, src3, waits=_clean(waits), sem=sem)

    def dma_done(self, s):
        return (s, self.cnt[s], None)

    def emit_prologue(self, d, has_f0, has_f1):
        xin3 = d["xin"].rearrange("(c p) f -> p c f", p=128)
        xr3 = self.xr[:, :].rearrange("p (c f) -> p c f", f=1024)
        for i in range(4):
            self.dma("sync", xr3[:, i * 4:(i + 1) * 4, :],
                     xin3[:, i * 4:(i + 1) * 4, :])
        self.dma("sync", self.ident[:, :], d["ident"][:, :])
        self.dma("sync", self.onesb[:, :], d["ones_bf"][:, :])
        self.dma("sync", self.onesf[:, :], d["ones_f"][:, :])
        if has_f0:
            self.w_cast_load(self.rwT[:, :], d["rw_t"], 8, sem="dCn")
            self.dma("sync", self.esc_row[:, :], d["esc_row"][:, :])
        if has_f1:
            self.w_cast_load(self.gateT[:, :], d["gate_t"], 8, sem="dCn")
            self.dma("sync", self.eb_row[:, :], d["eb_row"][:, :])
        dx_all = self.dma_done("dX")
        dcn_all = self.dma_done("dCn")
        t_esc = t_eb = None
        if has_f0:
            tb = self.op("tensor", "matmul", self.px[:, 0:8], self.onesf[0:1, 0:128],
                         self.esc_row[0:1, 0:8], start=True, stop=True,
                         waits=(dx_all, dcn_all), inc="T")
            t_esc = self.op("vector", "tensor_scalar", self.scale_half[:, :],
                            self.px[:, 0:8], 0.5, None, ALU.mult,
                            waits=(tb,), inc="V")
        if has_f1:
            tb3 = self.op("tensor", "matmul", self.px[:, 8:16], self.onesf[0:1, 0:128],
                          self.eb_row[0:1, 0:8], start=True, stop=True,
                          waits=_clean((dx_all, dcn_all, t_esc)), inc="T")
            t_eb = self.op("vector", "tensor_copy", self.biasb[:, :], self.px[:, 8:16],
                           waits=(tb3,), inc="V")
        self.bias8_tok = self.op("vector", "memset", self.bias8[:, :], -8.0, inc="V")
        self.setg("px", t_esc, t_eb)
        self.setg("xr", dx_all)
        self.t_consts = _clean((dx_all, dcn_all, t_esc, t_eb))

    def emit_ln(self):
        v, s, t = "vector", "scalar", "tensor"
        xr = self.xr
        xr_guard = _clean(self.use("xr") + self.use("xn_scr"))
        s1_tok = None
        ar3_dummy = self.ar3[:, 0:1024]
        ar3_prior = self.use("ar3")
        for tile in range(NT):
            xt = xr[:, tile * 1024:(tile + 1) * 1024]
            s1_tok = self.op(s, "activation", ar3_dummy, xt, AF.Copy,
                             accum_out=self.ln_s1[:, tile:tile + 1],
                             waits=_clean(tuple(xr_guard) + tuple(ar3_prior)) if tile == 0 else (),
                             inc="S")
            self.op(v, "scalar_tensor_tensor", self.xn_scr[:, 0:1024],
                    xt, 1.0, xt, ALU.mult, ALU.mult,
                    accum_out=self.ln_ssq[:, tile:tile + 1],
                    waits=_clean(tuple(self.use("xn_scr")) + tuple(xr_guard)) if tile == 0 else ())
        self.guards.setdefault("ar3", []).append(s1_tok)
        self.op(v, "tensor_scalar", self.ln_t[:, :16], self.ln_ssq[:, :16],
                1.0 / 1024, None, ALU.mult, waits=(s1_tok,))
        t_lnm = self.op(v, "tensor_scalar", self.ln_m[:, :16], self.ln_s1[:, :16],
                        1.0 / 1024, None, ALU.mult, waits=(s1_tok,), inc="V")
        self.op(v, "tensor_mul", self.rsc[:, 0:16], self.ln_m[:, :16], self.ln_m[:, :16])
        self.op(v, "tensor_sub", self.ln_t[:, :16], self.ln_t[:, :16], self.rsc[:, 0:16])
        tv = self.op(v, "tensor_scalar", self.ln_t[:, :16], self.ln_t[:, :16],
                     1e-5, None, ALU.add, inc="V")
        self.op(s, "activation", self.ln_lt[:, :16], self.ln_t[:, :16], AF.Ln,
                waits=(tv,))
        ts = self.op(s, "activation", self.ln_rstd[:, :16], self.ln_lt[:, :16],
                     AF.Exp, scale=-0.5, inc="S")
        prev_tt = None
        prev_evac = None
        pt_guard = self.use("pt")
        xs = self.xn_scr[:, 0:1024]
        for tile in range(NT):
            xt = xr[:, tile * 1024:(tile + 1) * 1024]
            tv = self.op(v, "tensor_scalar", xs, xt,
                         self.ln_m[:, tile:tile + 1], self.ln_rstd[:, tile:tile + 1],
                         ALU.subtract, ALU.mult,
                         waits=_clean((ts, prev_tt)), fwaits=(t_lnm,), inc="V")
            tt = None
            for j in range(NC_D):
                tt = self.op(t, "transpose", self.pt[:, j * 128:(j + 1) * 128],
                             xs[:, j * 128:(j + 1) * 128], self.ident[:, :],
                             waits=_clean([tv, prev_evac, ("dX", 16, None)] +
                                          (pt_guard if tile == 0 else [])) if j == 0 else (),
                             inc="T" if j == NC_D - 1 else None)
            prev_tt = tt
            dst = _ap3(self.xnt, tile * 128, [[NC_D * 2048, 128], [2048, NC_D], [1, 128]])
            src = self.pt[:, 0:1024].rearrange("p (c f) -> p c f", f=128)
            prev_evac = self.op(v, "tensor_copy", dst, src, waits=(tt,), inc="V")
        self.setg("xn_scr", prev_tt)
        self.setg("pt", prev_evac)
        return prev_evac

    def emit_f0(self, d, experts=True, first=True):
        t, v, s = "tensor", "vector", "scalar"
        xnt_ok = self.emit_ln()
        prev_te = None
        pxg = self.use("px")
        comb_guard = self.use("comb")
        last_router = None
        for tile in range(NT):
            off = (tile % 4) * 128
            pxr = self.px[:, off:off + 8]
            wts = _clean([xnt_ok, prev_te] + (pxg if tile == 0 else []) +
                         list(self.t_consts))
            tm = None
            for c in range(NC_D):
                tm = self.op(t, "matmul", pxr,
                             self.xnt[:, c * 2048 + tile * 128: c * 2048 + tile * 128 + 128],
                             self.rwT[:, c * 8:(c + 1) * 8],
                             start=(c == 0), stop=(c == NC_D - 1),
                             waits=wts if c == 0 else (),
                             inc="T" if c == NC_D - 1 else None)
            rs = self.rsc
            rmx, rsum, rc = rs[:, 16:17], rs[:, 17:18], rs[:, 18:19]
            m1, m2, gs, gr = rs[:, 19:20], rs[:, 20:21], rs[:, 21:22], rs[:, 22:23]
            prb, msk, pm, ep, gbuf = (rs[:, 24:32], rs[:, 32:40], rs[:, 40:48],
                                      rs[:, 48:56], rs[:, 56:64])
            tv = self.op(v, "reduce_max", rmx, pxr, axis=AX.X, negate=True,
                         waits=(tm,), inc="V")
            te = self.op(s, "activation", prb, pxr, AF.Exp, bias=rmx,
                         accum_out=rsum, waits=(tv,), inc="S")
            prev_te = te
            t_rc = self.op(v, "reciprocal", rc, rsum, waits=(te,), inc="V")
            self.op(v, "tensor_scalar", prb, prb, rc, None, ALU.mult, fwaits=(t_rc,))
            t_m1 = self.op(v, "reduce_max", m1, prb, axis=AX.X, inc="V")
            self.op(v, "tensor_scalar", msk, prb, m1, None, ALU.is_ge, fwaits=(t_m1,))
            self.op(v, "scalar_tensor_tensor", pm, msk, -1e9, prb, ALU.mult, ALU.add)
            t_m2 = self.op(v, "reduce_max", m2, pm, axis=AX.X, inc="V")
            tv = self.op(v, "tensor_scalar", msk, prb, m2, None, ALU.is_ge,
                         fwaits=(t_m2,), inc="V")
            te = self.op(s, "activation", ep, prb, AF.Exp, waits=(tv,), inc="S")
            self.op(v, "tensor_mul", gbuf, ep, msk, waits=(te,))
            self.op(v, "reduce_sum", gs, gbuf, axis=AX.X)
            t_gr = self.op(v, "reciprocal", gr, gs, inc="V")
            self.op(v, "tensor_scalar", gbuf, gbuf, gr, None, ALU.mult, fwaits=(t_gr,))
            last_router = self.op(v, "tensor_mul", self.comb[:, tile * 8:tile * 8 + 8],
                                  gbuf, self.scale_half[:, :],
                                  waits=_clean(comb_guard) if tile == 0 else (),
                                  inc="V")
        self.setg("px", last_router)
        if not experts:
            self.setg("comb", last_router)
            return
        abank = self.bank_init(4)
        ybank = {0: self.use2("p2", 0), 1: self.use2("p2", 1)}
        w2_guard = self.use("ar1")
        h_guard = self.use("ar2")
        w1_guard = {i: self.use("ar3") for i in range(4)}
        ai = yi = 0
        last_evac = None
        b_last = None
        RING = ("dWa", "dWb", "dWc", "dWd")
        for e in range(8):
            pre_toks = {}
            for fc in range(4):
                slot = fc % 4
                w1dst = self.ar3[:, slot * 1024:(slot + 1) * 1024]
                if first:
                    w1src = d["w1_t"][e, :, fc * 128:(fc + 1) * 128]
                else:
                    w1src = d["w1_bf"][e, :, fc * 128:(fc + 1) * 128]
                pre_toks[fc] = self.w_cast_load(w1dst, w1src, 128,
                                                waits=w1_guard[slot], sem=RING[slot])
            w2toks = []
            for i4 in range(4):
                dst = self.ar1[:, i4 * 8192:(i4 + 1) * 8192]
                if first:
                    srcw = d["w2_t"][e, i4 * 1024:(i4 + 1) * 1024, :]
                    w2toks.append(self.w_cast_load(dst, srcw, 1024, waits=w2_guard))
                else:
                    srcw = d["w2_bf"][e, i4 * 1024:(i4 + 1) * 1024, :]
                    w2toks.append(self.w_cast_load(dst, srcw, 1024, waits=w2_guard,
                                                   queue="sync", sem="dW2h"))
            w2tok = w2toks[-1]
            if first:
                w2_store = []
                for i4 in range(4):
                    dst3 = d["w2_bf"][e, i4 * 1024:(i4 + 1) * 1024, :].rearrange(
                        "(c p) f -> p c f", p=128)
                    src3 = self.ar1[:, i4 * 8192:(i4 + 1) * 8192].rearrange(
                        "p (c f) -> p c f", f=1024)
                    w2st = self.dma("sync", dst3, src3,
                                    waits=(w2toks[-1],),
                                    sem="dStW2")
                    w2_store.append(w2st)
            else:
                w2_store = []
            for qtr in range(4):
                silu_last = None
                st_all = None
                for fc in range(32):
                    slot = fc % 4
                    w1dst = self.ar3[:, slot * 1024:(slot + 1) * 1024]
                    if first and qtr == 0:
                        w1src = d["w1_t"][e, :, fc * 128:(fc + 1) * 128]
                    else:
                        w1src = d["w1_bf"][e, :, fc * 128:(fc + 1) * 128]
                    if qtr == 0 and fc < 4:
                        w1tok = pre_toks[fc]
                    else:
                        w1tok = self.w_cast_load(w1dst, w1src, 128,
                                                 waits=w1_guard[slot],
                                                 sem=RING[slot])
                    pb, poff = self.A_banks[ai % 4]
                    psA = pb[:, poff:poff + 512]
                    wts = _clean((w1tok, abank[ai % 4], xnt_ok))
                    tm = None
                    for c in range(NC_D):
                        tm = self.op(t, "matmul", psA,
                                     w1dst[:, c * 128:(c + 1) * 128],
                                     self.xnt[:, c * 2048 + qtr * 512: c * 2048 + (qtr + 1) * 512],
                                     start=(c == 0), stop=(c == NC_D - 1),
                                     waits=wts if c == 0 else (),
                                     inc="T" if c == NC_D - 1 else None)
                    fi = fc
                    stok = self.op(s, "activation",
                                   self.ar2[:, fi * 512:(fi + 1) * 512],
                                   psA, AF.Silu,
                                   waits=_clean([tm] + (h_guard if fi == 0 else [])),
                                   inc="S")
                    silu_last = stok
                    abank[ai % 4] = stok
                    ai += 1
                    if first and qtr == 0:
                        dstc = d["w1_bf"][e, :, fc * 128:(fc + 1) * 128]
                        d3 = dstc.rearrange("(c p) f -> p c f", p=128)
                        s3 = w1dst.rearrange("p (c f) -> p c f", f=128)
                        sttok = self.dma("sync", d3, s3, waits=(w1tok,),
                                         sem="dSt" + str(slot))
                        w1_guard[slot] = [tm, sttok]
                    else:
                        w1_guard[slot] = [tm]
                for tt in range(4):
                    gt = qtr * 4 + tt
                    for dh in range(2):
                        psY = self.p2[:, (yi % 2) * 512:(yi % 2) * 512 + 512]
                        wts = _clean((silu_last, w2tok, ybank[yi % 2]))
                        tm = None
                        for fi in range(32):
                            tm = self.op(t, "matmul", psY,
                                         self.ar2[:, fi * 512 + tt * 128: fi * 512 + tt * 128 + 128],
                                         self.ar1[:, fi * 1024 + dh * 512: fi * 1024 + dh * 512 + 512],
                                         start=(fi == 0), stop=(fi == 31),
                                         waits=wts if fi == 0 else (),
                                         inc="T" if fi == 31 else None)
                        b_last = tm
                        xsl = self.xr[:, gt * 1024 + dh * 512: gt * 1024 + dh * 512 + 512]
                        last_evac = self.op(v, "scalar_tensor_tensor", xsl, psY,
                                            self.comb[:, gt * 8 + e: gt * 8 + e + 1],
                                            xsl, ALU.mult, ALU.add,
                                            waits=_clean((tm,)),
                                            fwaits=(last_router,), inc="V")
                        ybank[yi % 2] = last_evac
                        yi += 1
                h_guard = [b_last]
            w2_guard = [b_last] + ([self.dma_done("dStW2")] if w2_store else [])
        self.setg("ar1", b_last,
                  self.dma_done("dStW2") if first else None)
        self.setg("ar2", b_last)
        self.setg("ar3", *(w1_guard[0] + w1_guard[1] + w1_guard[2] + w1_guard[3]))
        self.setg("xr", last_evac)
        self.setg("comb", last_evac)
        self.setg("p0p1", *[abank[i] for i in range(4)])
        self.set2("p2", 0, ybank[0])
        self.set2("p2", 1, ybank[1])

    def emit_attn(self, d, layer):
        t, v, s = "tensor", "vector", "scalar"
        is_l0 = (layer == "A0")
        xnt_ok = self.emit_ln()
        QT0, KT0, V0, CT0, WO0 = 0, 8192, 16384, 24704, 26752
        ET = [self.ar2[:, 0:8192], self.ar2[:, 8192:16384]]
        OT0 = 2048
        self.sbank = self.bank_init(4)
        self.bi = 0
        ar1_guard = self.use("ar1")
        ar2_guard = self.use("ar2")
        ar3_guard = self.use("ar3")
        px_readers = _clean(self.use("px"))
        ctx_tok = None
        if is_l0:
            wetok = self.w_cast_load(self.ar3[:, 0:1024], d["we_t"], 128,
                                     waits=ar3_guard, sem="dWa")
            ar3_guard = []
            qkg = {0: [wetok], 1: []}
            for qc in range(4):
                pb, poff = self.A_banks[self.bi % 4]
                psc = pb[:, poff:poff + 512]
                wts = _clean([wetok, self.sbank[self.bi % 4], xnt_ok])
                tm = None
                for c in range(NC_D):
                    tm = self.op(t, "matmul", psc,
                                 self.ar3[:, c * 128:(c + 1) * 128],
                                 self.xnt[:, c * 2048 + qc * 512: c * 2048 + (qc + 1) * 512],
                                 start=(c == 0), stop=(c == NC_D - 1),
                                 waits=wts if c == 0 else (),
                                 inc="T" if c == NC_D - 1 else None)
                ctx_tok = self.op(s, "activation",
                                  self.ar1[:, CT0 + qc * 512: CT0 + (qc + 1) * 512],
                                  psc, AF.Copy, scale=SQ01,
                                  waits=_clean([tm] + (ar1_guard if qc == 0 else [])),
                                  inc="S")
                self.sbank[self.bi % 4] = ctx_tok
                self.bi += 1
            qk_stream_guard = {0: [ctx_tok], 1: []}
        else:
            qk_stream_guard = {0: ar3_guard, 1: ar3_guard}
        last_wo_mm = None
        xr_guard = self.use("xr")
        last_xr_add = xr_guard[-1] if xr_guard else None
        ot_guard = self.use("ar3")
        pv_last = {}
        for grp in range(2):
            wv_dst = self.ar2[:, 0:4096]
            if is_l0:
                wv_src = d["wv_t"][:, grp * 512:(grp + 1) * 512]
            else:
                wv_src = d["qkv_w"][:, 2048 + grp * 512: 2048 + (grp + 1) * 512]
            wvtok = self.w_cast_load(wv_dst, wv_src, 512, waits=ar2_guard, sem="dWv")
            qkvb = self.ar2[0:1, 8192:11264]
            qbtok = None
            if not is_l0:
                qbtok = self.dma("gpsimd", qkvb, d["qkv_b"][:, :], waits=ar2_guard, sem="dQB")
            qkv_last = None
            for which, base, scale in (("q", QT0, 0.125), ("k", KT0, 1.0)):
                for j in range(4):
                    mcol = grp * 512 + j * 128
                    par = j % 2
                    wdst = self.ar3[:, par * 1024:(par + 1) * 1024]
                    if is_l0:
                        wsrc = d["wq_t" if which == "q" else "wk_t"][:, mcol:mcol + 128]
                    else:
                        wofs = 0 if which == "q" else 1024
                        wsrc = d["qkv_w"][:, wofs + mcol: wofs + mcol + 128]
                    wtok = self.w_cast_load(wdst, wsrc, 128,
                                            waits=qk_stream_guard[par],
                                            sem=("dWa" if par == 0 else "dWb"))
                    tm = None
                    for nch in range(4):
                        pb, poff = self.A_banks[self.bi % 4]
                        psq = pb[:, poff:poff + 512]
                        wts = _clean((wtok, self.sbank[self.bi % 4], xnt_ok))
                        for c in range(NC_D):
                            last_of_group = (c == NC_D - 1) and is_l0
                            tm = self.op(t, "matmul", psq,
                                         wdst[:, c * 128:(c + 1) * 128],
                                         self.xnt[:, c * 2048 + nch * 512: c * 2048 + (nch + 1) * 512],
                                         start=(c == 0), stop=last_of_group,
                                         waits=wts if c == 0 else (),
                                         inc="T" if last_of_group else None)
                        if not is_l0:
                            bofs = (0 if which == "q" else 1024) + mcol
                            tm = self.op(t, "matmul", psq,
                                         qkvb[0:1, bofs:bofs + 128],
                                         self.onesb[0:1, 0:512],
                                         start=False, stop=True,
                                         waits=self.t_consts + (qbtok,), inc="T")
                        etok = self.op(s, "activation",
                                       self.ar1[:, base + j * 2048 + nch * 512:
                                                base + j * 2048 + nch * 512 + 512],
                                       psq, AF.Copy, scale=scale,
                                       waits=_clean([tm] + (ar1_guard if self.bi < 4 else [])),
                                       inc="S")
                        qkv_last = etok
                        self.sbank[self.bi % 4] = etok
                        self.bi += 1
                    qk_stream_guard[par] = [qkv_last]
            pbtok = None
            if (not is_l0) and grp == 1:
                pbtok = self.dma("gpsimd", self.ar3[0:1, 0:1024], d["proj_b"][:, :],
                                 waits=_clean(qk_stream_guard[0]), sem="dPB")
            v_last = None
            vmm_last = None
            for tt in range(NT):
                half = tt % 2
                psv = self.p2[:, half * 512:half * 512 + 512]
                wts = _clean((wvtok, self.use2("p2", half), xnt_ok))
                tm = None
                for c in range(NC_D):
                    stop = (c == NC_D - 1) and is_l0
                    tm = self.op(t, "matmul", psv,
                                 self.xnt[:, c * 2048 + tt * 128: c * 2048 + tt * 128 + 128],
                                 wv_dst[:, c * 512:(c + 1) * 512],
                                 start=(c == 0), stop=stop,
                                 waits=wts if c == 0 else (),
                                 inc="T" if stop else None)
                if not is_l0:
                    tm = self.op(t, "matmul", psv,
                                 self.onesb[0:1, 0:128],
                                 qkvb[0:1, 2048 + grp * 512: 2048 + (grp + 1) * 512],
                                 start=False, stop=True, waits=(qbtok,), inc="T")
                vmm_last = tm
                dst = _ap3(self.ar1, V0 + tt * 520, [[32768, 128], [65, 8], [1, 64]])
                src = psv.rearrange("p (h f) -> p h f", f=64)
                vtok = self.op(v, "tensor_copy", dst, src,
                               waits=_clean([tm] + (ar1_guard if tt == 0 else [])),
                               inc="V")
                onescol = _ap3(self.ar1, V0 + tt * 520 + 64, [[32768, 128], [65, 8], [1, 1]])
                v_last = self.op(v, "memset", onescol, 1.0, inc="V")
                self.set2("p2", half, v_last)
            wo_src = (d["wo_t"] if is_l0 else d["proj_w"])[grp * 512:(grp + 1) * 512, :]
            wotok = self.w_cast_load(self.ar1[:, WO0:WO0 + 4096], wo_src, 1024,
                                     waits=_clean([last_wo_mm] +
                                                  (ar1_guard if grp == 0 else [])),
                                     sem="dWo")
            et_guard = {0: ar2_guard + [vmm_last], 1: ar2_guard + [vmm_last]}
            ar2_guard = []
            for qb in range(4):
                mult_toks = {}
                exp_done = {}

                def emit_S(l, qb=qb):
                    nonlocal et_guard
                    par = l % 2
                    prow = (l % 2) * 64
                    jt = l // 2
                    exp_last = None
                    self.bi += self.bi % 2
                    for kt in range(NT):
                        pb, poff = self.A_banks[self.bi % 4]
                        pss = pb[:, poff:poff + 512]
                        wts = _clean([self.sbank[self.bi % 4], qkv_last, v_last,
                                      ctx_tok if is_l0 else None])
                        if is_l0:
                            self.op(t, "matmul", pss,
                                    self.ar1[prow:prow + 64,
                                             KT0 + jt * 2048 + kt * 128: KT0 + jt * 2048 + kt * 128 + 128],
                                    self.ar1[prow:prow + 64,
                                             QT0 + jt * 2048 + qb * 512: QT0 + jt * 2048 + (qb + 1) * 512],
                                    start=True, stop=False,
                                    tile_position=(prow, 0),
                                    waits=wts)
                            tm = self.op(t, "matmul", pss,
                                         self.ar1[:, CT0 + kt * 128: CT0 + kt * 128 + 128],
                                         self.ar1[:, CT0 + qb * 512: CT0 + (qb + 1) * 512],
                                         start=False, stop=True, inc="T")
                        else:
                            tm = self.op(t, "matmul", pss,
                                         self.ar1[prow:prow + 64,
                                                  KT0 + jt * 2048 + kt * 128: KT0 + jt * 2048 + kt * 128 + 128],
                                         self.ar1[prow:prow + 64,
                                                  QT0 + jt * 2048 + qb * 512: QT0 + jt * 2048 + (qb + 1) * 512],
                                         start=True, stop=True,
                                         tile_position=(prow, 0),
                                         waits=wts, inc="T")
                        if kt % 2 == 1:
                            pairbase = (self.bi - 1) % 4
                            tmi = self.op(s, "activation",
                                          ET[par][:, (kt - 1) * 512:(kt + 1) * 512],
                                          pb[:, poff - 512:poff + 512], AF.Exp,
                                          bias=(self.bias8[:, :] if is_l0 else 0.0),
                                          waits=_clean([tm, pv_last.get(par),
                                                        self.bias8_tok if is_l0 else None] +
                                                       (et_guard[par] if kt == 1 else [])),
                                          inc="S")
                            exp_last = tmi
                            self.sbank[pairbase] = tmi
                            self.sbank[self.bi % 4] = tmi
                        self.bi += 1
                    et_guard[par] = []
                    exp_done[l] = exp_last

                def emit_PV(l, qb=qb):
                    nonlocal ot_guard
                    par = l % 2
                    prow = (l % 2) * 64
                    jt = l // 2
                    wts = _clean([exp_done[l], v_last] + list(self.use("px")))
                    tmm = None
                    for kt in range(NT):
                        tmm = self.op(t, "matmul", self.px[0:65, 0:512],
                                      _ap3(self.ar1, V0 + kt * 520 + l * 65,
                                           [[32768, 128], [1, 65]]),
                                      ET[par][:, kt * 512:(kt + 1) * 512],
                                      start=(kt == 0), stop=(kt == NT - 1),
                                      waits=wts if kt == 0 else (),
                                      inc="T" if kt == NT - 1 else None)
                    pv_last[par] = tmm
                    ot_sl = self.ar3[prow:prow + 64,
                                     OT0 + jt * 512: OT0 + jt * 512 + 512]
                    ucp = self.op(s, "activation", ot_sl, self.px[0:64, 0:512],
                                  AF.Copy,
                                  waits=_clean([tmm] + (ot_guard if l < 2 else [])),
                                  inc="S")
                    rcp = self.op(v, "tensor_copy", self.rrow[:, :], self.px[64:65, 0:512],
                                  waits=(tmm,), inc="V")
                    rrc = self.op(v, "reciprocal", self.rrow[:, :], self.rrow[:, :],
                                  inc="V")
                    bc = self.op(t, "matmul", self.px[64:128, 0:512],
                                 self.onesf[0:1, 0:64], self.rrow[0:1, 0:512],
                                 start=True, stop=True,
                                 waits=_clean((rrc, ucp)), inc="T")
                    mt = self.op(v, "tensor_mul", ot_sl, ot_sl,
                                 self.px[64:128, 0:512],
                                 waits=(bc,), inc="V")
                    mult_toks[l] = mt
                    self.setg("px", mt, ucp, rcp)

                emit_S(0)
                for l in range(8):
                    if l < 7:
                        emit_S(l + 1)
                    emit_PV(l)
                ot_guard = []
                wo_toks = []
                for qt in range(4):
                    gt = qb * 4 + qt
                    for dh in range(2):
                        psr = self.p2[:, dh * 512:(dh + 1) * 512]
                        wts = _clean((mult_toks[7], wotok, self.use2("p2", dh)))
                        tm = None
                        add_bias = (not is_l0) and grp == 1
                        for j in range(4):
                            stop = (j == 3) and not add_bias
                            tm = self.op(t, "matmul", psr,
                                         self.ar3[:, OT0 + j * 512 + qt * 128:
                                                  OT0 + j * 512 + qt * 128 + 128],
                                         self.ar1[:, WO0 + j * 1024 + dh * 512:
                                                  WO0 + j * 1024 + dh * 512 + 512],
                                         start=(j == 0), stop=stop,
                                         waits=wts if j == 0 else (),
                                         inc="T" if stop else None)
                        if add_bias:
                            tm = self.op(t, "matmul", psr,
                                         self.onesb[0:1, 0:128],
                                         self.ar3[0:1, dh * 512:(dh + 1) * 512],
                                         start=False, stop=True,
                                         waits=_clean((pbtok,)), inc="T")
                        wo_toks.append(tm)
                        xsl = self.xr[:, gt * 1024 + dh * 512: gt * 1024 + dh * 512 + 512]
                        last_xr_add = self.op(v, "tensor_add", xsl, xsl, psr,
                                              waits=_clean((tm, last_xr_add)), inc="V")
                        self.set2("p2", dh, last_xr_add)
                last_wo_mm = wo_toks[-1]
                ot_guard = [last_wo_mm]
            ar2_guard = list(_clean([pv_last.get(0), pv_last.get(1)]))
        self.setg("ar1", last_wo_mm, pv_last.get(0), pv_last.get(1))
        self.setg("ar2", pv_last.get(0), pv_last.get(1))
        self.setg("ar3", last_wo_mm)
        self.setg("xr", last_xr_add)
        self.setg("p0p1", *[self.sbank[i] for i in range(4)])

    def emit_f1(self, d, expert_set=tuple(range(9)), first=True):
        t, v, s = "tensor", "vector", "scalar"
        xnt_ok = self.emit_ln()

        prev_te = None
        pxg = self.use("px")
        comb_guard = self.use("comb")
        last_gate = None
        for tile in range(NT):
            off = (tile % 4) * 128
            pxr = self.px[:, off:off + 8]
            wts = _clean([xnt_ok, prev_te] + (list(pxg) if tile == 0 else []) +
                         list(self.t_consts))
            tm = None
            for c in range(NC_D):
                tm = self.op(t, "matmul", pxr,
                             self.xnt[:, c * 2048 + tile * 128: c * 2048 + tile * 128 + 128],
                             self.gateT[:, c * 8:(c + 1) * 8],
                             start=(c == 0), stop=(c == NC_D - 1),
                             waits=wts if c == 0 else (),
                             inc="T" if c == NC_D - 1 else None)
            rs = self.rsc
            sco, bia, msk, pm, gsel = (rs[:, 24:32], rs[:, 32:40], rs[:, 40:48],
                                       rs[:, 48:56], rs[:, 56:64])
            m1, m2 = rs[:, 16:17], rs[:, 17:18]
            te = self.op(s, "activation", sco, pxr, AF.Sigmoid,
                         waits=_clean((tm, last_gate)), inc="S")
            prev_te = te
            self.op(v, "tensor_add", bia, sco, self.biasb[:, :], waits=(te,))
            t_m1 = self.op(v, "reduce_max", m1, bia, axis=AX.X, inc="V")
            self.op(v, "tensor_scalar", msk, bia, m1, None, ALU.is_ge, fwaits=(t_m1,))
            self.op(v, "scalar_tensor_tensor", pm, msk, -1e9, bia, ALU.mult, ALU.add)
            t_m2 = self.op(v, "reduce_max", m2, pm, axis=AX.X, inc="V")
            self.op(v, "tensor_scalar", msk, bia, m2, None, ALU.is_ge, fwaits=(t_m2,))
            self.op(v, "tensor_mul", gsel, msk, sco)
            last_gate = self.op(v, "scalar_tensor_tensor",
                                self.comb[:, tile * 8:tile * 8 + 8],
                                gsel, 0.5, gsel, ALU.mult, ALU.mult,
                                waits=_clean(comb_guard) if tile == 0 else (),
                                inc="V")
        self.setg("px", last_gate)
        abank = self.bank_init(4)
        ybank = {0: self.use2("p2", 0), 1: self.use2("p2", 1)}
        wd_guard = self.use("ar2")
        h_guard = self.use("ar1")
        wu_guard = {i: self.use("ar3") for i in range(4)}
        ai = yi = 0
        last_evac = None
        b_last = None
        RING = ("dWa", "dWb", "dWc", "dWd")

        def wu_src(e, fc):
            if first:
                return (d["wu_t"][e, :, fc * 128:(fc + 1) * 128] if e < 8
                        else d["swu_t"][:, fc * 128:(fc + 1) * 128])
            return (d["wu_bf"][e, :, fc * 128:(fc + 1) * 128] if e < 8
                    else d["swu_bf"][:, fc * 128:(fc + 1) * 128])

        for e in expert_set:
            pre_toks = {}
            for fc in range(4):
                par = fc % 4
                wudst = self.ar3[:, par * 1024:(par + 1) * 1024]
                pre_toks[fc] = self.w_cast_load(wudst, wu_src(e, fc), 128,
                                                waits=wu_guard[par], sem=RING[par])
            wdsrc = (d["wd_t"][e] if e < 8 else d["swd_t"]) if first else                     (d["wd_bf"][e] if e < 8 else d["swd_bf"])
            wdtoks = []
            for (r0, r1) in ((0, 6), (6, 11)):
                dst = self.ar2[:, r0 * 1024: r1 * 1024]
                src = wdsrc[r0 * 128: r1 * 128, :]
                wdtoks.append(self.w_cast_load(dst, src, 1024, waits=wd_guard,
                                               queue="gpsimd" if first else "sync",
                                               sem="dW2" if first else "dW2h"))
            wdtok = wdtoks[-1]
            wd_store = []
            if first:
                wdbf = d["wd_bf"][e] if e < 8 else d["swd_bf"]
                for (r0, r1) in ((0, 6), (6, 11)):
                    dst3 = wdbf[r0 * 128: r1 * 128, :].rearrange("(c p) f -> p c f", p=128)
                    src3 = self.ar2[:, r0 * 1024: r1 * 1024].rearrange(
                        "p (c f) -> p c f", f=1024)
                    wd_store.append(self.dma("sync", dst3, src3,
                                             waits=(wdtok,), sem="dStW2"))
            sq_last = None
            for fc in range(11):
                par = fc % 4
                wudst = self.ar3[:, par * 1024:(par + 1) * 1024]
                if fc < 4:
                    wutok = pre_toks[fc]
                else:
                    wutok = self.w_cast_load(wudst, wu_src(e, fc), 128,
                                             waits=wu_guard[par], sem=RING[par])
                for tc in range(4):
                    pb, poff = self.A_banks[ai % 4]
                    psA = pb[:, poff:poff + 512]
                    wts = _clean((wutok, abank[ai % 4], xnt_ok))
                    tm = None
                    for c in range(NC_D):
                        tm = self.op(t, "matmul", psA,
                                     wudst[:, c * 128:(c + 1) * 128],
                                     self.xnt[:, c * 2048 + tc * 512: c * 2048 + (tc + 1) * 512],
                                     start=(c == 0), stop=(c == NC_D - 1),
                                     waits=wts if c == 0 else (),
                                     inc="T" if c == NC_D - 1 else None)
                    hsl = self.ar1[:, fc * 2048 + tc * 512: fc * 2048 + (tc + 1) * 512]
                    rtok = self.op(s, "activation", hsl, psA, AF.Relu,
                                   waits=_clean([tm] + (h_guard if fc == 0 and tc == 0 else [])),
                                   inc="S")
                    sq_last = self.op(v, "tensor_mul", hsl, hsl, hsl,
                                      waits=(rtok,), inc="V")
                    abank[ai % 4] = sq_last
                    ai += 1
                if first:
                    wubf = d["wu_bf"][e, :, fc * 128:(fc + 1) * 128] if e < 8 \
                        else d["swu_bf"][:, fc * 128:(fc + 1) * 128]
                    d3 = wubf.rearrange("(c p) f -> p c f", p=128)
                    s3 = wudst.rearrange("p (c f) -> p c f", f=128)
                    sttok = self.dma("sync", d3, s3, waits=(wutok,),
                                     sem="dSt" + str(par))
                    wu_guard[par] = [tm, sttok]
                else:
                    wu_guard[par] = [tm]
            for tt in range(NT):
                for dh in range(2):
                    psY = self.p2[:, (yi % 2) * 512:(yi % 2) * 512 + 512]
                    wts = _clean((sq_last, wdtok, ybank[yi % 2]))
                    tm = None
                    for fc in range(11):
                        tm = self.op(t, "matmul", psY,
                                     self.ar1[:, fc * 2048 + tt * 128: fc * 2048 + tt * 128 + 128],
                                     self.ar2[:, fc * 1024 + dh * 512: fc * 1024 + dh * 512 + 512],
                                     start=(fc == 0), stop=(fc == 10),
                                     waits=wts if fc == 0 else (),
                                     inc="T" if fc == 10 else None)
                    b_last = tm
                    xsl = self.xr[:, tt * 1024 + dh * 512: tt * 1024 + dh * 512 + 512]
                    scl = (self.comb[:, tt * 8 + e: tt * 8 + e + 1] if e < 8 else 0.5)
                    last_evac = self.op(v, "scalar_tensor_tensor", xsl, psY,
                                        scl, xsl, ALU.mult, ALU.add,
                                        waits=_clean((tm,)),
                                        fwaits=(last_gate,) if e < 8 else (),
                                        inc="V")
                    ybank[yi % 2] = last_evac
                    yi += 1
            h_guard = [b_last]
            wd_guard = [b_last] + ([self.dma_done("dStW2")] if wd_store else [])
        self.setg("ar1", b_last)
        self.setg("ar2", b_last,
                  self.dma_done("dStW2") if first else None)
        self.setg("ar3", *(wu_guard[0] + wu_guard[1] + wu_guard[2] + wu_guard[3]))
        self.setg("xr", last_evac)
        self.setg("comb", last_evac)
        self.setg("p0p1", *[abank[i] for i in range(4)])
        self.set2("p2", 0, ybank[0])
        self.set2("p2", 1, ybank[1])

    def barrier(self):
        toks = []
        for eng, sem in (("tensor", "T"), ("vector", "V"), ("scalar", "S")):
            toks.append(self.op(eng, "nop", inc=sem))
        for s in self.DMA_SEMS:
            toks.append(self.dma_done(s))
        for eng in ("tensor", "vector", "scalar", "gpsimd", "sync"):
            for tk in toks:
                self.wait_force(eng, tk)

    def emit_epilogue(self, d):
        xr_done = _clean(self.use("xr"))
        xo3 = d["xout"].rearrange("(c p) f -> p c f", p=128)
        xr3 = self.xr[:, :].rearrange("p (c f) -> p c f", f=1024)
        for i in range(4):
            self.dma("sync", xo3[:, i * 4:(i + 1) * 4, :],
                     xr3[:, i * 4:(i + 1) * 4, :], waits=xr_done)
        self.wait("sync", self.dma_done("dX"))

    def build(self, d):
        self.emit_prologue(d, "F0" in self.layers, "F1" in self.layers)
        f0_seen = 0
        f1_seen = 0
        for l in self.layers:
            if l == "F0":
                self.emit_f0(d, first=(f0_seen == 0))
                f0_seen += 1
            elif l == "A0":
                self.emit_attn(d, "A0")
            elif l == "A1":
                self.emit_attn(d, "A1")
            elif l == "F1":
                self.emit_f1(d, first=(f1_seen == 0))
                f1_seen += 1
        self.emit_epilogue(d)
        with self.nc.Block() as blk:
            for name, reg in (("tensor", blk.tensor), ("vector", blk.vector),
                              ("scalar", blk.scalar), ("gpsimd", blk.gpsimd),
                              ("sync", blk.sync)):
                ops = self.prog[name]

                def body(e, ops=ops):
                    for f in ops:
                        f(e)
                reg(body)


def _declare_inputs(nc, layers):
    d = {}

    def dt(name, shape, dtype=F32, kind="ExternalInput"):
        d[name] = nc.dram_tensor(name, shape, dtype, kind=kind).ap()
    dt("xin", [2048, 1024])
    dt("xout", [2048, 1024], kind="ExternalOutput")
    dt("ident", [128, 128], BF16)
    dt("ones_bf", [1, 512], BF16)
    dt("ones_f", [1, 128], F32)
    if "F0" in layers:
        dt("rw_t", [1024, 8])
        dt("esc_row", [1, 8])
        dt("w1_t", [8, 1024, 4096])
        dt("w2_t", [8, 4096, 1024])
        dt("w1_bf", [8, 1024, 4096], BF16, kind="Internal")
        dt("w2_bf", [8, 4096, 1024], BF16, kind="Internal")
    if "A0" in layers:
        dt("wq_t", [1024, 1024])
        dt("wk_t", [1024, 1024])
        dt("wv_t", [1024, 1024])
        dt("wo_t", [1024, 1024])
        dt("we_t", [1024, 128])
    if "A1" in layers:
        dt("qkv_w", [1024, 3072])
        dt("proj_w", [1024, 1024])
    if "A1" in layers:
        dt("qkv_b", [1, 3072])
        dt("proj_b", [1, 1024])
    if "F1" in layers:
        dt("gate_t", [1024, 8])
        dt("eb_row", [1, 8])
        dt("wu_t", [8, 1024, 1408])
        dt("wd_t", [8, 1408, 1024])
        dt("swu_t", [1024, 1408])
        dt("swd_t", [1408, 1024])
        dt("wu_bf", [8, 1024, 1408], BF16, kind="Internal")
        dt("wd_bf", [8, 1408, 1024], BF16, kind="Internal")
        dt("swu_bf", [1024, 1408], BF16, kind="Internal")
        dt("swd_bf", [1408, 1024], BF16, kind="Internal")
    return d


def build_nc(layers, tag=""):
    nc = bass.Bass("TRN2", target_bir_lowering=False, debug=False,
                   dynamic_dma_scratch_size=16384)
    d = _declare_inputs(nc, layers)
    mk = MK(nc, layers, tag=tag)
    ctx = ExitStack()
    with ctx:
        mk.alloc(ctx, has_l1=("A1" in layers or "F1" in layers))
        mk.build(d)
    return nc


def _host_weights(inputs):
    c = np.ascontiguousarray
    w = {}
    w["rw_t"] = c(inputs["L0_ffn_router_w"].T)
    w["esc_row"] = c(inputs["L0_ffn_expert_scale"].reshape(1, 8))
    w["w1_t"] = c(inputs["L0_ffn_w1"].transpose(0, 2, 1))
    w["w2_t"] = c(inputs["L0_ffn_w2"].transpose(0, 2, 1))
    w["wq_t"] = c(inputs["L0_attn_Wq"].T)
    w["wk_t"] = c(inputs["L0_attn_Wk"].T)
    w["wv_t"] = c(inputs["L0_attn_Wv"].T)
    w["wo_t"] = c(inputs["L0_attn_Wo"].T)
    w["we_t"] = c(inputs["L0_attn_We"].T)
    w["qkv_w"] = c(inputs["L1_attn_qkv_w"])
    w["qkv_b"] = c(inputs["L1_attn_qkv_b"].reshape(1, 3072))
    w["proj_w"] = c(inputs["L1_attn_proj_w"])
    w["proj_b"] = c(inputs["L1_attn_proj_b"].reshape(1, 1024))
    w["gate_t"] = c(inputs["L1_ffn_gate_w"].T)
    w["eb_row"] = c(inputs["L1_ffn_expert_bias"].reshape(1, 8))
    w["wu_t"] = c(inputs["L1_ffn_w_up"].transpose(0, 2, 1))
    w["wd_t"] = c(inputs["L1_ffn_w_down"].transpose(0, 2, 1))
    w["swu_t"] = c(inputs["L1_ffn_shared_w_up"].T)
    w["swd_t"] = c(inputs["L1_ffn_shared_w_down"].T)
    import ml_dtypes
    w["ident"] = np.eye(128, dtype=ml_dtypes.bfloat16)
    w["ones_bf"] = np.ones((1, 512), dtype=ml_dtypes.bfloat16)
    w["ones_f"] = np.ones((1, 128), dtype=np.float32)
    return w


def _needed_names(layers):
    names = ["ident", "ones_bf", "ones_f"]
    if "F0" in layers:
        names += ["rw_t", "esc_row", "w1_t", "w2_t"]
    if "A0" in layers:
        names += ["wq_t", "wk_t", "wv_t", "wo_t", "we_t"]
    if "A1" in layers:
        names += ["qkv_w", "proj_w", "qkv_b", "proj_b"]
    if "F1" in layers:
        names += ["gate_t", "eb_row", "wu_t", "wd_t", "swu_t", "swd_t"]
    return names


def run_layers(x, inputs, layers, n_cores=8):
    w = _host_weights(inputs)
    nc = build_nc(layers)
    need = _needed_names(layers)
    in_maps = []
    for cidx in range(n_cores):
        m = {"xin": np.ascontiguousarray(x[cidx])}
        for k in need:
            m[k] = w[k]
        in_maps.append(m)
    res = run_bass_kernel_spmd(nc, in_maps, core_ids=list(range(n_cores)))
    return np.stack([np.asarray(r["xout"]) for r in res.results], axis=0)


def kernel(**inputs):
    x = np.ascontiguousarray(np.asarray(inputs["L0_attn_X"], dtype=np.float32))
    out = run_layers(x, inputs, list(ALL_LAYERS), n_cores=8)
    return out.astype(np.float32)


# version 12
# speedup vs baseline: 1.1549x; 1.0533x over previous
import numpy as np
from contextlib import ExitStack

import concourse.bass as bass
import concourse.mybir as mybir
from concourse.bass_utils import run_bass_kernel_spmd

F32 = mybir.dt.float32
BF16 = mybir.dt.bfloat16
AF = mybir.ActivationFunctionType
ALU = mybir.AluOpType
AX = mybir.AxisListType

NT = 16
NC_D = 8
SQ01 = float(np.sqrt(0.1))

ALL_LAYERS = ("F0", "A0", "F0", "F1", "A1", "F1")


def _ap3(t, off, dims):
    return bass.AP(t, off, dims)


def _clean(toks):
    return tuple(x for x in toks if x is not None)


class MK:
    DMA_SEMS = ("dX", "dW2", "dWa", "dWb", "dWc", "dWd", "dWv", "dWo", "dQB", "dPB", "dCn", "dSt0", "dSt1", "dSt2", "dSt3", "dStW2")

    def __init__(self, nc, layers, tag=""):
        self.nc = nc
        self.layers = layers
        self.tag = tag
        self.prog = {e: [] for e in ("tensor", "vector", "scalar", "gpsimd", "sync")}
        self.cnt = {}
        self.semh = {}
        self.hiwater = {e: {} for e in self.prog}
        self.guards = {}

    def sem(self, name, ctx):
        self.semh[name] = ctx.enter_context(self.nc.semaphore(self.tag + name))
        self.cnt[name] = 0

    def raw(self, eng, fn):
        self.prog[eng].append(fn)

    def wait(self, eng, tok):
        if tok is None:
            return
        sname, val, producer = tok
        if producer == eng:
            return
        if self.hiwater[eng].get(sname, 0) >= val:
            return
        self.hiwater[eng][sname] = val
        h = self.semh[sname]
        self.raw(eng, lambda e, h=h, v=val: e.wait_ge(h, v))

    def wait_force(self, eng, tok):
        if tok is None:
            return
        sname, val, producer = tok
        if self.hiwater[eng].get(sname, 0) >= val:
            return
        self.hiwater[eng][sname] = val
        h = self.semh[sname]
        self.raw(eng, lambda e, h=h, v=val: e.wait_ge(h, v))

    AUTO_CHAIN = {"vector": "V", "scalar": "S"}

    def op(self, eng, method, *args, waits=(), fwaits=(), inc=None, **kw):
        auto = self.AUTO_CHAIN.get(eng)
        if auto is not None and method != "dma_start":
            self.wait_force(eng, (auto, self.cnt[auto], None))
        for t in waits:
            self.wait(eng, t)
        for t in fwaits:
            self.wait_force(eng, t)
        inc_list = []
        tok = None
        if inc is not None:
            name, k = (inc, 1) if isinstance(inc, str) else inc
            self.cnt[name] += k
            prod = None if name in self.DMA_SEMS else eng
            tok = (name, self.cnt[name], prod)
            inc_list = [(self.semh[name], k)]
        if auto is not None and method != "dma_start":
            if inc is None:
                self.cnt[auto] += 1
                tok = (auto, self.cnt[auto], eng)
                inc_list = [(self.semh[auto], 1)]
            else:
                assert inc == auto or (isinstance(inc, tuple) and inc[0] == auto), \
                    f"{eng} op with inc {inc}"

        def cl(e, method=method, args=args, kw=kw, inc_list=inc_list):
            ins = getattr(e, method)(*args, **kw)
            for (h, k) in inc_list:
                ins.then_inc(h, k)
        self.raw(eng, cl)
        return tok

    def dma(self, queue, out, in_, waits=(), sem="dX"):
        return self.op(queue, "dma_start", out, in_, waits=waits, inc=(sem, 16))

    def use(self, buf):
        return list(self.guards.get(buf, ()))

    def setg(self, buf, *toks):
        self.guards[buf] = [t for t in toks if t is not None]

    def use2(self, name, idx):
        return self.guards.get((name, idx), None)

    def set2(self, name, idx, tok):
        self.guards[(name, idx)] = tok

    def bank_init(self, n):
        prev = self.use("p0p1")
        return {i: (prev[i] if i < len(prev) else (prev[-1] if prev else None))
                for i in range(n)}

    def alloc(self, ctx, has_l1=True):
        nc = self.nc
        sb = lambda n, s, d: ctx.enter_context(nc.sbuf_tensor(self.tag + "sb_" + n, s, d))
        ps = lambda n, s, d: ctx.enter_context(nc.psum_tensor(self.tag + n, s, d))
        self.xr = sb("xr", [128, NT * 1024], F32)
        self.xnt = sb("xnt", [128, NC_D * 2048], BF16)
        self.ar1 = sb("ar1", [128, 32768], BF16)
        self.ar2 = sb("ar2", [128, 16384], BF16)
        self.ar3 = sb("ar3", [128, 4096], BF16)
        self.ident = sb("ident", [128, 128], BF16)
        self.onesb = sb("onesb", [1, 512], BF16)
        self.onesf = sb("onesf", [1, 128], F32)
        self.rwT = sb("rwT", [128, 64], BF16)
        self.gateT = sb("gateT", [128, 64], BF16)
        self.esc_row = sb("esc_row", [1, 8], F32)
        self.eb_row = sb("eb_row", [1, 8], F32)
        self.scale_half = sb("scale_half", [128, 8], F32)
        self.biasb = sb("biasb", [128, 8], F32)
        self.comb = sb("comb", [128, NT * 8], F32)
        self.ln_s1 = sb("ln_s1", [128, 16], F32)
        self.ln_ssq = sb("ln_ssq", [128, 16], F32)
        self.ln_m = sb("ln_m", [128, 16], F32)
        self.ln_t = sb("ln_t", [128, 16], F32)
        self.ln_lt = sb("ln_lt", [128, 16], F32)
        self.ln_rstd = sb("ln_rstd", [128, 16], F32)
        self.rsc = sb("rsc", [128, 64], F32)
        self.xn_scr = sb("xn_scr", [128, 1024], BF16)
        self.rrow = sb("rrow", [1, 512], F32)
        self.bias8 = sb("bias8", [128, 1], F32)

        self.p0 = ps("p0", [128, 1024], F32)
        self.p1 = ps("p1", [128, 1024], F32)
        self.p2 = ps("p2", [128, 1024], F32)
        self.px = ps("px", [128, 512], F32)
        self.pt = ps("pt", [128, 1024], BF16)
        self.A_banks = [(self.p0, 0), (self.p0, 512), (self.p1, 0), (self.p1, 512)]
        for s in self.DMA_SEMS + ("T", "S", "V"):
            self.sem(s, ctx)

    def w_cast_load(self, dst2d, src2d, cols_per_chunk, waits=(), sem="dW2"):
        dst3 = dst2d.rearrange("p (c f) -> p c f", f=cols_per_chunk)
        src3 = src2d.rearrange("(c p) f -> p c f", p=128)
        return self.dma("gpsimd", dst3, src3, waits=_clean(waits), sem=sem)

    def dma_done(self, s):
        return (s, self.cnt[s], None)

    def emit_prologue(self, d, has_f0, has_f1):
        xin3 = d["xin"].rearrange("(c p) f -> p c f", p=128)
        xr3 = self.xr[:, :].rearrange("p (c f) -> p c f", f=1024)
        for i in range(4):
            self.dma("sync", xr3[:, i * 4:(i + 1) * 4, :],
                     xin3[:, i * 4:(i + 1) * 4, :])
        self.dma("sync", self.ident[:, :], d["ident"][:, :])
        self.dma("sync", self.onesb[:, :], d["ones_bf"][:, :])
        self.dma("sync", self.onesf[:, :], d["ones_f"][:, :])
        if has_f0:
            self.w_cast_load(self.rwT[:, :], d["rw_t"], 8, sem="dCn")
            self.dma("sync", self.esc_row[:, :], d["esc_row"][:, :])
        if has_f1:
            self.w_cast_load(self.gateT[:, :], d["gate_t"], 8, sem="dCn")
            self.dma("sync", self.eb_row[:, :], d["eb_row"][:, :])
        dx_all = self.dma_done("dX")
        dcn_all = self.dma_done("dCn")
        t_esc = t_eb = None
        if has_f0:
            tb = self.op("tensor", "matmul", self.px[:, 0:8], self.onesf[0:1, 0:128],
                         self.esc_row[0:1, 0:8], start=True, stop=True,
                         waits=(dx_all, dcn_all), inc="T")
            t_esc = self.op("vector", "tensor_scalar", self.scale_half[:, :],
                            self.px[:, 0:8], 0.5, None, ALU.mult,
                            waits=(tb,), inc="V")
        if has_f1:
            tb3 = self.op("tensor", "matmul", self.px[:, 8:16], self.onesf[0:1, 0:128],
                          self.eb_row[0:1, 0:8], start=True, stop=True,
                          waits=_clean((dx_all, dcn_all, t_esc)), inc="T")
            t_eb = self.op("vector", "tensor_copy", self.biasb[:, :], self.px[:, 8:16],
                           waits=(tb3,), inc="V")
        self.bias8_tok = self.op("vector", "memset", self.bias8[:, :], -8.0, inc="V")
        self.setg("px", t_esc, t_eb)
        self.setg("xr", dx_all)
        self.t_consts = _clean((dx_all, dcn_all, t_esc, t_eb))

    def emit_ln(self):
        v, s, t = "vector", "scalar", "tensor"
        xr = self.xr
        xr_guard = _clean(self.use("xr") + self.use("xn_scr"))
        s1_tok = None
        ar3_dummy = self.ar3[:, 0:1024]
        ar3_prior = self.use("ar3")
        for tile in range(NT):
            xt = xr[:, tile * 1024:(tile + 1) * 1024]
            s1_tok = self.op(s, "activation", ar3_dummy, xt, AF.Copy,
                             accum_out=self.ln_s1[:, tile:tile + 1],
                             waits=_clean(tuple(xr_guard) + tuple(ar3_prior)) if tile == 0 else (),
                             inc="S")
            self.op(v, "scalar_tensor_tensor", self.xn_scr[:, 0:1024],
                    xt, 1.0, xt, ALU.mult, ALU.mult,
                    accum_out=self.ln_ssq[:, tile:tile + 1],
                    waits=_clean(tuple(self.use("xn_scr")) + tuple(xr_guard)) if tile == 0 else ())
        self.guards.setdefault("ar3", []).append(s1_tok)
        self.op(v, "tensor_scalar", self.ln_t[:, :16], self.ln_ssq[:, :16],
                1.0 / 1024, None, ALU.mult, waits=(s1_tok,))
        t_lnm = self.op(v, "tensor_scalar", self.ln_m[:, :16], self.ln_s1[:, :16],
                        1.0 / 1024, None, ALU.mult, waits=(s1_tok,), inc="V")
        self.op(v, "tensor_mul", self.rsc[:, 0:16], self.ln_m[:, :16], self.ln_m[:, :16])
        self.op(v, "tensor_sub", self.ln_t[:, :16], self.ln_t[:, :16], self.rsc[:, 0:16])
        tv = self.op(v, "tensor_scalar", self.ln_t[:, :16], self.ln_t[:, :16],
                     1e-5, None, ALU.add, inc="V")
        self.op(s, "activation", self.ln_lt[:, :16], self.ln_t[:, :16], AF.Ln,
                waits=(tv,))
        ts = self.op(s, "activation", self.ln_rstd[:, :16], self.ln_lt[:, :16],
                     AF.Exp, scale=-0.5, inc="S")
        prev_tt = None
        prev_evac = None
        evac_toks = []
        pt_guard = self.use("pt")
        xs = self.xn_scr[:, 0:1024]
        for tile in range(NT):
            xt = xr[:, tile * 1024:(tile + 1) * 1024]
            tv = self.op(v, "tensor_scalar", xs, xt,
                         self.ln_m[:, tile:tile + 1], self.ln_rstd[:, tile:tile + 1],
                         ALU.subtract, ALU.mult,
                         waits=_clean((ts, prev_tt)), fwaits=(t_lnm,), inc="V")
            tt = None
            for j in range(NC_D):
                tt = self.op(t, "transpose", self.pt[:, j * 128:(j + 1) * 128],
                             xs[:, j * 128:(j + 1) * 128], self.ident[:, :],
                             waits=_clean([tv, prev_evac, ("dX", 16, None)] +
                                          (pt_guard if tile == 0 else [])) if j == 0 else (),
                             inc="T" if j == NC_D - 1 else None)
            prev_tt = tt
            dst = _ap3(self.xnt, tile * 128, [[NC_D * 2048, 128], [2048, NC_D], [1, 128]])
            src = self.pt[:, 0:1024].rearrange("p (c f) -> p c f", f=128)
            prev_evac = self.op(v, "tensor_copy", dst, src, waits=(tt,), inc="V")
            evac_toks.append(prev_evac)
        self.setg("xn_scr", prev_tt)
        self.setg("pt", prev_evac)
        return prev_evac, evac_toks

    def emit_f0(self, d, experts=True, first=True):
        t, v, s = "tensor", "vector", "scalar"
        xnt_ok, xnt_tl = self.emit_ln()
        prev_te = None
        pxg = self.use("px")
        comb_guard = self.use("comb")
        last_router = None
        for tile in range(NT):
            off = (tile % 4) * 128
            pxr = self.px[:, off:off + 8]
            wts = _clean([xnt_tl[tile], prev_te] + (pxg if tile == 0 else []) +
                         list(self.t_consts))
            tm = None
            for c in range(NC_D):
                tm = self.op(t, "matmul", pxr,
                             self.xnt[:, c * 2048 + tile * 128: c * 2048 + tile * 128 + 128],
                             self.rwT[:, c * 8:(c + 1) * 8],
                             start=(c == 0), stop=(c == NC_D - 1),
                             waits=wts if c == 0 else (),
                             inc="T" if c == NC_D - 1 else None)
            rs = self.rsc
            rmx, rsum, rc = rs[:, 16:17], rs[:, 17:18], rs[:, 18:19]
            m1, m2, gs, gr = rs[:, 19:20], rs[:, 20:21], rs[:, 21:22], rs[:, 22:23]
            prb, msk, pm, ep, gbuf = (rs[:, 24:32], rs[:, 32:40], rs[:, 40:48],
                                      rs[:, 48:56], rs[:, 56:64])
            tv = self.op(v, "reduce_max", rmx, pxr, axis=AX.X, negate=True,
                         waits=(tm,), inc="V")
            te = self.op(s, "activation", prb, pxr, AF.Exp, bias=rmx,
                         accum_out=rsum, waits=(tv,), inc="S")
            prev_te = te
            t_rc = self.op(v, "reciprocal", rc, rsum, waits=(te,), inc="V")
            self.op(v, "tensor_scalar", prb, prb, rc, None, ALU.mult, fwaits=(t_rc,))
            t_m1 = self.op(v, "reduce_max", m1, prb, axis=AX.X, inc="V")
            self.op(v, "tensor_scalar", msk, prb, m1, None, ALU.is_ge, fwaits=(t_m1,))
            self.op(v, "scalar_tensor_tensor", pm, msk, -1e9, prb, ALU.mult, ALU.add)
            t_m2 = self.op(v, "reduce_max", m2, pm, axis=AX.X, inc="V")
            tv = self.op(v, "tensor_scalar", msk, prb, m2, None, ALU.is_ge,
                         fwaits=(t_m2,), inc="V")
            te = self.op(s, "activation", ep, prb, AF.Exp, waits=(tv,), inc="S")
            self.op(v, "tensor_mul", gbuf, ep, msk, waits=(te,))
            self.op(v, "reduce_sum", gs, gbuf, axis=AX.X)
            t_gr = self.op(v, "reciprocal", gr, gs, inc="V")
            self.op(v, "tensor_scalar", gbuf, gbuf, gr, None, ALU.mult, fwaits=(t_gr,))
            last_router = self.op(v, "tensor_mul", self.comb[:, tile * 8:tile * 8 + 8],
                                  gbuf, self.scale_half[:, :],
                                  waits=_clean(comb_guard) if tile == 0 else (),
                                  inc="V")
        self.setg("px", last_router)
        if not experts:
            self.setg("comb", last_router)
            return
        abank = self.bank_init(4)
        ybank = {0: self.use2("p2", 0), 1: self.use2("p2", 1)}
        w2_guard = self.use("ar1")
        h_guard = self.use("ar2")
        w1_guard = {i: self.use("ar3") for i in range(4)}
        ai = yi = 0
        last_evac = None
        b_last = None
        RING = ("dWa", "dWb", "dWc", "dWd")
        for e in range(8):
            pre_toks = {}
            for fc in range(4):
                slot = fc % 4
                w1dst = self.ar3[:, slot * 1024:(slot + 1) * 1024]
                if first:
                    w1src = d["w1_t"][e, :, fc * 128:(fc + 1) * 128]
                else:
                    w1src = d["w1_bf"][e, :, fc * 128:(fc + 1) * 128]
                pre_toks[fc] = self.w_cast_load(w1dst, w1src, 128,
                                                waits=w1_guard[slot], sem=RING[slot])
            w2toks = []
            for i4 in range(4):
                dst = self.ar1[:, i4 * 8192:(i4 + 1) * 8192]
                if first:
                    srcw = d["w2_t"][e, i4 * 1024:(i4 + 1) * 1024, :]
                    w2toks.append(self.w_cast_load(dst, srcw, 1024, waits=w2_guard))
                else:
                    srcw = d["w2_bf"][e, i4 * 1024:(i4 + 1) * 1024, :]
                    w2toks.append(self.w_cast_load(dst, srcw, 1024, waits=w2_guard))
            w2tok = w2toks[-1]
            if first:
                w2_store = []
                for i4 in range(4):
                    dst3 = d["w2_bf"][e, i4 * 1024:(i4 + 1) * 1024, :].rearrange(
                        "(c p) f -> p c f", p=128)
                    src3 = self.ar1[:, i4 * 8192:(i4 + 1) * 8192].rearrange(
                        "p (c f) -> p c f", f=1024)
                    w2st = self.dma("sync", dst3, src3,
                                    waits=(w2toks[-1],),
                                    sem="dStW2")
                    w2_store.append(w2st)
            else:
                w2_store = []
            for qtr in range(4):
                silu_last = None
                st_all = None
                for fc in range(32):
                    slot = fc % 4
                    w1dst = self.ar3[:, slot * 1024:(slot + 1) * 1024]
                    if first and qtr == 0:
                        w1src = d["w1_t"][e, :, fc * 128:(fc + 1) * 128]
                    else:
                        w1src = d["w1_bf"][e, :, fc * 128:(fc + 1) * 128]
                    if qtr == 0 and fc < 4:
                        w1tok = pre_toks[fc]
                    else:
                        w1tok = self.w_cast_load(w1dst, w1src, 128,
                                                 waits=w1_guard[slot],
                                                 sem=RING[slot])
                    pb, poff = self.A_banks[ai % 4]
                    psA = pb[:, poff:poff + 512]
                    wts = _clean((w1tok, abank[ai % 4], xnt_tl[qtr * 4 + 3]))
                    tm = None
                    for c in range(NC_D):
                        tm = self.op(t, "matmul", psA,
                                     w1dst[:, c * 128:(c + 1) * 128],
                                     self.xnt[:, c * 2048 + qtr * 512: c * 2048 + (qtr + 1) * 512],
                                     start=(c == 0), stop=(c == NC_D - 1),
                                     waits=wts if c == 0 else (),
                                     inc="T" if c == NC_D - 1 else None)
                    fi = fc
                    stok = self.op(s, "activation",
                                   self.ar2[:, fi * 512:(fi + 1) * 512],
                                   psA, AF.Silu,
                                   waits=_clean([tm] + (h_guard if fi == 0 else [])),
                                   inc="S")
                    silu_last = stok
                    abank[ai % 4] = stok
                    ai += 1
                    if first and qtr == 0:
                        dstc = d["w1_bf"][e, :, fc * 128:(fc + 1) * 128]
                        d3 = dstc.rearrange("(c p) f -> p c f", p=128)
                        s3 = w1dst.rearrange("p (c f) -> p c f", f=128)
                        sttok = self.dma("sync", d3, s3, waits=(w1tok,),
                                         sem="dSt" + str(slot))
                        w1_guard[slot] = [tm, sttok]
                    else:
                        w1_guard[slot] = [tm]
                for tt in range(4):
                    gt = qtr * 4 + tt
                    for dh in range(2):
                        psY = self.p2[:, (yi % 2) * 512:(yi % 2) * 512 + 512]
                        wts = _clean((silu_last, w2tok, ybank[yi % 2]))
                        tm = None
                        for fi in range(32):
                            tm = self.op(t, "matmul", psY,
                                         self.ar2[:, fi * 512 + tt * 128: fi * 512 + tt * 128 + 128],
                                         self.ar1[:, fi * 1024 + dh * 512: fi * 1024 + dh * 512 + 512],
                                         start=(fi == 0), stop=(fi == 31),
                                         waits=wts if fi == 0 else (),
                                         inc="T" if fi == 31 else None)
                        b_last = tm
                        xsl = self.xr[:, gt * 1024 + dh * 512: gt * 1024 + dh * 512 + 512]
                        last_evac = self.op(v, "scalar_tensor_tensor", xsl, psY,
                                            self.comb[:, gt * 8 + e: gt * 8 + e + 1],
                                            xsl, ALU.mult, ALU.add,
                                            waits=_clean((tm,)),
                                            fwaits=(last_router,), inc="V")
                        ybank[yi % 2] = last_evac
                        yi += 1
                h_guard = [b_last]
            w2_guard = [b_last] + ([self.dma_done("dStW2")] if w2_store else [])
        self.setg("ar1", b_last,
                  self.dma_done("dStW2") if first else None)
        self.setg("ar2", b_last)
        self.setg("ar3", *(w1_guard[0] + w1_guard[1] + w1_guard[2] + w1_guard[3]))
        self.setg("xr", last_evac)
        self.setg("comb", last_evac)
        self.setg("p0p1", *[abank[i] for i in range(4)])
        self.set2("p2", 0, ybank[0])
        self.set2("p2", 1, ybank[1])

    def emit_attn(self, d, layer):
        t, v, s = "tensor", "vector", "scalar"
        is_l0 = (layer == "A0")
        xnt_ok, xnt_tl = self.emit_ln()
        QT0, KT0, V0, CT0, WO0 = 0, 8192, 16384, 24704, 26752
        ET = [self.ar2[:, 0:8192], self.ar2[:, 8192:16384]]
        OT0 = 2048
        self.sbank = self.bank_init(4)
        self.bi = 0
        ar1_guard = self.use("ar1")
        ar2_guard = self.use("ar2")
        ar3_guard = self.use("ar3")
        px_readers = _clean(self.use("px"))
        ctx_tok = None
        if is_l0:
            wetok = self.w_cast_load(self.ar3[:, 0:1024], d["we_t"], 128,
                                     waits=ar3_guard, sem="dWa")
            ar3_guard = []
            qkg = {0: [wetok], 1: []}
            for qc in range(4):
                pb, poff = self.A_banks[self.bi % 4]
                psc = pb[:, poff:poff + 512]
                wts = _clean([wetok, self.sbank[self.bi % 4], xnt_tl[qc * 4 + 3]])
                tm = None
                for c in range(NC_D):
                    tm = self.op(t, "matmul", psc,
                                 self.ar3[:, c * 128:(c + 1) * 128],
                                 self.xnt[:, c * 2048 + qc * 512: c * 2048 + (qc + 1) * 512],
                                 start=(c == 0), stop=(c == NC_D - 1),
                                 waits=wts if c == 0 else (),
                                 inc="T" if c == NC_D - 1 else None)
                ctx_tok = self.op(s, "activation",
                                  self.ar1[:, CT0 + qc * 512: CT0 + (qc + 1) * 512],
                                  psc, AF.Copy, scale=SQ01,
                                  waits=_clean([tm] + (ar1_guard if qc == 0 else [])),
                                  inc="S")
                self.sbank[self.bi % 4] = ctx_tok
                self.bi += 1
            qk_stream_guard = {0: [ctx_tok], 1: []}
        else:
            qk_stream_guard = {0: ar3_guard, 1: ar3_guard}
        last_wo_mm = None
        xr_guard = self.use("xr")
        last_xr_add = xr_guard[-1] if xr_guard else None
        ot_guard = self.use("ar3")
        pv_last = {}
        for grp in range(2):
            wv_dst = self.ar2[:, 0:4096]
            if is_l0:
                wv_src = d["wv_t"][:, grp * 512:(grp + 1) * 512]
            else:
                wv_src = d["qkv_w"][:, 2048 + grp * 512: 2048 + (grp + 1) * 512]
            wvtok = self.w_cast_load(wv_dst, wv_src, 512, waits=ar2_guard, sem="dWv")
            qkvb = self.ar2[0:1, 8192:11264]
            qbtok = None
            if not is_l0:
                qbtok = self.dma("gpsimd", qkvb, d["qkv_b"][:, :], waits=ar2_guard, sem="dQB")
            qkv_last = None
            for which, base, scale in (("q", QT0, 0.125), ("k", KT0, 1.0)):
                for j in range(4):
                    mcol = grp * 512 + j * 128
                    par = j % 2
                    wdst = self.ar3[:, par * 1024:(par + 1) * 1024]
                    if is_l0:
                        wsrc = d["wq_t" if which == "q" else "wk_t"][:, mcol:mcol + 128]
                    else:
                        wofs = 0 if which == "q" else 1024
                        wsrc = d["qkv_w"][:, wofs + mcol: wofs + mcol + 128]
                    wtok = self.w_cast_load(wdst, wsrc, 128,
                                            waits=qk_stream_guard[par],
                                            sem=("dWa" if par == 0 else "dWb"))
                    tm = None
                    for nch in range(4):
                        pb, poff = self.A_banks[self.bi % 4]
                        psq = pb[:, poff:poff + 512]
                        wts = _clean((wtok, self.sbank[self.bi % 4],
                                      xnt_tl[nch * 4 + 3]))
                        for c in range(NC_D):
                            last_of_group = (c == NC_D - 1) and is_l0
                            tm = self.op(t, "matmul", psq,
                                         wdst[:, c * 128:(c + 1) * 128],
                                         self.xnt[:, c * 2048 + nch * 512: c * 2048 + (nch + 1) * 512],
                                         start=(c == 0), stop=last_of_group,
                                         waits=wts if c == 0 else (),
                                         inc="T" if last_of_group else None)
                        if not is_l0:
                            bofs = (0 if which == "q" else 1024) + mcol
                            tm = self.op(t, "matmul", psq,
                                         qkvb[0:1, bofs:bofs + 128],
                                         self.onesb[0:1, 0:512],
                                         start=False, stop=True,
                                         waits=self.t_consts + (qbtok,), inc="T")
                        etok = self.op(s, "activation",
                                       self.ar1[:, base + j * 2048 + nch * 512:
                                                base + j * 2048 + nch * 512 + 512],
                                       psq, AF.Copy, scale=scale,
                                       waits=_clean([tm] + (ar1_guard if self.bi < 4 else [])),
                                       inc="S")
                        qkv_last = etok
                        self.sbank[self.bi % 4] = etok
                        self.bi += 1
                    qk_stream_guard[par] = [qkv_last]
            pbtok = None
            if (not is_l0) and grp == 1:
                pbtok = self.dma("gpsimd", self.ar3[0:1, 0:1024], d["proj_b"][:, :],
                                 waits=_clean(qk_stream_guard[0]), sem="dPB")
            v_last = None
            vmm_last = None
            for tt in range(NT):
                half = tt % 2
                psv = self.p2[:, half * 512:half * 512 + 512]
                wts = _clean((wvtok, self.use2("p2", half), xnt_tl[tt]))
                tm = None
                for c in range(NC_D):
                    stop = (c == NC_D - 1) and is_l0
                    tm = self.op(t, "matmul", psv,
                                 self.xnt[:, c * 2048 + tt * 128: c * 2048 + tt * 128 + 128],
                                 wv_dst[:, c * 512:(c + 1) * 512],
                                 start=(c == 0), stop=stop,
                                 waits=wts if c == 0 else (),
                                 inc="T" if stop else None)
                if not is_l0:
                    tm = self.op(t, "matmul", psv,
                                 self.onesb[0:1, 0:128],
                                 qkvb[0:1, 2048 + grp * 512: 2048 + (grp + 1) * 512],
                                 start=False, stop=True, waits=(qbtok,), inc="T")
                vmm_last = tm
                dst = _ap3(self.ar1, V0 + tt * 520, [[32768, 128], [65, 8], [1, 64]])
                src = psv.rearrange("p (h f) -> p h f", f=64)
                vtok = self.op(v, "tensor_copy", dst, src,
                               waits=_clean([tm] + (ar1_guard if tt == 0 else [])),
                               inc="V")
                onescol = _ap3(self.ar1, V0 + tt * 520 + 64, [[32768, 128], [65, 8], [1, 1]])
                v_last = self.op(v, "memset", onescol, 1.0, inc="V")
                self.set2("p2", half, v_last)
            wo_src = (d["wo_t"] if is_l0 else d["proj_w"])[grp * 512:(grp + 1) * 512, :]
            wotok = self.w_cast_load(self.ar1[:, WO0:WO0 + 4096], wo_src, 1024,
                                     waits=_clean([last_wo_mm] +
                                                  (ar1_guard if grp == 0 else [])),
                                     sem="dWo")
            et_guard = {0: ar2_guard + [vmm_last], 1: ar2_guard + [vmm_last]}
            ar2_guard = []
            for qb in range(4):
                mult_toks = {}
                exp_done = {}

                def emit_S(l, qb=qb):
                    nonlocal et_guard
                    par = l % 2
                    prow = (l % 2) * 64
                    jt = l // 2
                    exp_last = None
                    self.bi += self.bi % 2
                    for kt in range(NT):
                        pb, poff = self.A_banks[self.bi % 4]
                        pss = pb[:, poff:poff + 512]
                        wts = _clean([self.sbank[self.bi % 4], qkv_last, v_last,
                                      ctx_tok if is_l0 else None])
                        if is_l0:
                            self.op(t, "matmul", pss,
                                    self.ar1[prow:prow + 64,
                                             KT0 + jt * 2048 + kt * 128: KT0 + jt * 2048 + kt * 128 + 128],
                                    self.ar1[prow:prow + 64,
                                             QT0 + jt * 2048 + qb * 512: QT0 + jt * 2048 + (qb + 1) * 512],
                                    start=True, stop=False,
                                    tile_position=(prow, 0),
                                    waits=wts)
                            tm = self.op(t, "matmul", pss,
                                         self.ar1[:, CT0 + kt * 128: CT0 + kt * 128 + 128],
                                         self.ar1[:, CT0 + qb * 512: CT0 + (qb + 1) * 512],
                                         start=False, stop=True, inc="T")
                        else:
                            tm = self.op(t, "matmul", pss,
                                         self.ar1[prow:prow + 64,
                                                  KT0 + jt * 2048 + kt * 128: KT0 + jt * 2048 + kt * 128 + 128],
                                         self.ar1[prow:prow + 64,
                                                  QT0 + jt * 2048 + qb * 512: QT0 + jt * 2048 + (qb + 1) * 512],
                                         start=True, stop=True,
                                         tile_position=(prow, 0),
                                         waits=wts, inc="T")
                        if kt % 2 == 1:
                            pairbase = (self.bi - 1) % 4
                            tmi = self.op(s, "activation",
                                          ET[par][:, (kt - 1) * 512:(kt + 1) * 512],
                                          pb[:, poff - 512:poff + 512], AF.Exp,
                                          bias=(self.bias8[:, :] if is_l0 else 0.0),
                                          waits=_clean([tm, pv_last.get(par),
                                                        self.bias8_tok if is_l0 else None] +
                                                       (et_guard[par] if kt == 1 else [])),
                                          inc="S")
                            exp_last = tmi
                            self.sbank[pairbase] = tmi
                            self.sbank[self.bi % 4] = tmi
                        self.bi += 1
                    et_guard[par] = []
                    exp_done[l] = exp_last

                def emit_PV(l, qb=qb):
                    nonlocal ot_guard
                    par = l % 2
                    prow = (l % 2) * 64
                    jt = l // 2
                    wts = _clean([exp_done[l], v_last] + list(self.use("px")))
                    tmm = None
                    for kt in range(NT):
                        tmm = self.op(t, "matmul", self.px[0:65, 0:512],
                                      _ap3(self.ar1, V0 + kt * 520 + l * 65,
                                           [[32768, 128], [1, 65]]),
                                      ET[par][:, kt * 512:(kt + 1) * 512],
                                      start=(kt == 0), stop=(kt == NT - 1),
                                      waits=wts if kt == 0 else (),
                                      inc="T" if kt == NT - 1 else None)
                    pv_last[par] = tmm
                    ot_sl = self.ar3[prow:prow + 64,
                                     OT0 + jt * 512: OT0 + jt * 512 + 512]
                    ucp = self.op(s, "activation", ot_sl, self.px[0:64, 0:512],
                                  AF.Copy,
                                  waits=_clean([tmm] + (ot_guard if l < 2 else [])),
                                  inc="S")
                    rcp = self.op(v, "tensor_copy", self.rrow[:, :], self.px[64:65, 0:512],
                                  waits=(tmm,), inc="V")
                    rrc = self.op(v, "reciprocal", self.rrow[:, :], self.rrow[:, :],
                                  inc="V")
                    bc = self.op(t, "matmul", self.px[64:128, 0:512],
                                 self.onesf[0:1, 0:64], self.rrow[0:1, 0:512],
                                 start=True, stop=True,
                                 waits=_clean((rrc, ucp)), inc="T")
                    mt = self.op(v, "tensor_mul", ot_sl, ot_sl,
                                 self.px[64:128, 0:512],
                                 waits=(bc,), inc="V")
                    mult_toks[l] = mt
                    self.setg("px", mt, ucp, rcp)

                emit_S(0)
                for l in range(8):
                    if l < 7:
                        emit_S(l + 1)
                    emit_PV(l)
                ot_guard = []
                wo_toks = []
                for qt in range(4):
                    gt = qb * 4 + qt
                    for dh in range(2):
                        psr = self.p2[:, dh * 512:(dh + 1) * 512]
                        wts = _clean((mult_toks[7], wotok, self.use2("p2", dh)))
                        tm = None
                        add_bias = (not is_l0) and grp == 1
                        for j in range(4):
                            stop = (j == 3) and not add_bias
                            tm = self.op(t, "matmul", psr,
                                         self.ar3[:, OT0 + j * 512 + qt * 128:
                                                  OT0 + j * 512 + qt * 128 + 128],
                                         self.ar1[:, WO0 + j * 1024 + dh * 512:
                                                  WO0 + j * 1024 + dh * 512 + 512],
                                         start=(j == 0), stop=stop,
                                         waits=wts if j == 0 else (),
                                         inc="T" if stop else None)
                        if add_bias:
                            tm = self.op(t, "matmul", psr,
                                         self.onesb[0:1, 0:128],
                                         self.ar3[0:1, dh * 512:(dh + 1) * 512],
                                         start=False, stop=True,
                                         waits=_clean((pbtok,)), inc="T")
                        wo_toks.append(tm)
                        xsl = self.xr[:, gt * 1024 + dh * 512: gt * 1024 + dh * 512 + 512]
                        last_xr_add = self.op(v, "tensor_add", xsl, xsl, psr,
                                              waits=_clean((tm, last_xr_add)), inc="V")
                        self.set2("p2", dh, last_xr_add)
                last_wo_mm = wo_toks[-1]
                ot_guard = [last_wo_mm]
            ar2_guard = list(_clean([pv_last.get(0), pv_last.get(1)]))
        self.setg("ar1", last_wo_mm, pv_last.get(0), pv_last.get(1))
        self.setg("ar2", pv_last.get(0), pv_last.get(1))
        self.setg("ar3", last_wo_mm)
        self.setg("xr", last_xr_add)
        self.setg("p0p1", *[self.sbank[i] for i in range(4)])

    def emit_f1(self, d, expert_set=tuple(range(9)), first=True):
        t, v, s = "tensor", "vector", "scalar"
        xnt_ok, xnt_tl = self.emit_ln()

        prev_te = None
        pxg = self.use("px")
        comb_guard = self.use("comb")
        last_gate = None
        for tile in range(NT):
            off = (tile % 4) * 128
            pxr = self.px[:, off:off + 8]
            wts = _clean([xnt_tl[tile], prev_te] + (list(pxg) if tile == 0 else []) +
                         list(self.t_consts))
            tm = None
            for c in range(NC_D):
                tm = self.op(t, "matmul", pxr,
                             self.xnt[:, c * 2048 + tile * 128: c * 2048 + tile * 128 + 128],
                             self.gateT[:, c * 8:(c + 1) * 8],
                             start=(c == 0), stop=(c == NC_D - 1),
                             waits=wts if c == 0 else (),
                             inc="T" if c == NC_D - 1 else None)
            rs = self.rsc
            sco, bia, msk, pm, gsel = (rs[:, 24:32], rs[:, 32:40], rs[:, 40:48],
                                       rs[:, 48:56], rs[:, 56:64])
            m1, m2 = rs[:, 16:17], rs[:, 17:18]
            te = self.op(s, "activation", sco, pxr, AF.Sigmoid,
                         waits=_clean((tm, last_gate)), inc="S")
            prev_te = te
            self.op(v, "tensor_add", bia, sco, self.biasb[:, :], waits=(te,))
            t_m1 = self.op(v, "reduce_max", m1, bia, axis=AX.X, inc="V")
            self.op(v, "tensor_scalar", msk, bia, m1, None, ALU.is_ge, fwaits=(t_m1,))
            self.op(v, "scalar_tensor_tensor", pm, msk, -1e9, bia, ALU.mult, ALU.add)
            t_m2 = self.op(v, "reduce_max", m2, pm, axis=AX.X, inc="V")
            self.op(v, "tensor_scalar", msk, bia, m2, None, ALU.is_ge, fwaits=(t_m2,))
            self.op(v, "tensor_mul", gsel, msk, sco)
            last_gate = self.op(v, "scalar_tensor_tensor",
                                self.comb[:, tile * 8:tile * 8 + 8],
                                gsel, 0.5, gsel, ALU.mult, ALU.mult,
                                waits=_clean(comb_guard) if tile == 0 else (),
                                inc="V")
        self.setg("px", last_gate)
        abank = self.bank_init(4)
        ybank = {0: self.use2("p2", 0), 1: self.use2("p2", 1)}
        wd_guard = self.use("ar2")
        h_guard = self.use("ar1")
        wu_guard = {i: self.use("ar3") for i in range(4)}
        ai = yi = 0
        last_evac = None
        b_last = None
        RING = ("dWa", "dWb", "dWc", "dWd")

        def wu_src(e, fc):
            if first:
                return (d["wu_t"][e, :, fc * 128:(fc + 1) * 128] if e < 8
                        else d["swu_t"][:, fc * 128:(fc + 1) * 128])
            return (d["wu_bf"][e, :, fc * 128:(fc + 1) * 128] if e < 8
                    else d["swu_bf"][:, fc * 128:(fc + 1) * 128])

        for e in expert_set:
            pre_toks = {}
            for fc in range(4):
                par = fc % 4
                wudst = self.ar3[:, par * 1024:(par + 1) * 1024]
                pre_toks[fc] = self.w_cast_load(wudst, wu_src(e, fc), 128,
                                                waits=wu_guard[par], sem=RING[par])
            wdsrc = (d["wd_t"][e] if e < 8 else d["swd_t"]) if first else                     (d["wd_bf"][e] if e < 8 else d["swd_bf"])
            wdtoks = []
            for (r0, r1) in ((0, 6), (6, 11)):
                dst = self.ar2[:, r0 * 1024: r1 * 1024]
                src = wdsrc[r0 * 128: r1 * 128, :]
                wdtoks.append(self.w_cast_load(dst, src, 1024, waits=wd_guard))
            wdtok = wdtoks[-1]
            wd_store = []
            if first:
                wdbf = d["wd_bf"][e] if e < 8 else d["swd_bf"]
                for (r0, r1) in ((0, 6), (6, 11)):
                    dst3 = wdbf[r0 * 128: r1 * 128, :].rearrange("(c p) f -> p c f", p=128)
                    src3 = self.ar2[:, r0 * 1024: r1 * 1024].rearrange(
                        "p (c f) -> p c f", f=1024)
                    wd_store.append(self.dma("sync", dst3, src3,
                                             waits=(wdtok,), sem="dStW2"))
            sq_last = None
            for fc in range(11):
                par = fc % 4
                wudst = self.ar3[:, par * 1024:(par + 1) * 1024]
                if fc < 4:
                    wutok = pre_toks[fc]
                else:
                    wutok = self.w_cast_load(wudst, wu_src(e, fc), 128,
                                             waits=wu_guard[par], sem=RING[par])
                for tc in range(4):
                    pb, poff = self.A_banks[ai % 4]
                    psA = pb[:, poff:poff + 512]
                    wts = _clean((wutok, abank[ai % 4], xnt_tl[tc * 4 + 3]))
                    tm = None
                    for c in range(NC_D):
                        tm = self.op(t, "matmul", psA,
                                     wudst[:, c * 128:(c + 1) * 128],
                                     self.xnt[:, c * 2048 + tc * 512: c * 2048 + (tc + 1) * 512],
                                     start=(c == 0), stop=(c == NC_D - 1),
                                     waits=wts if c == 0 else (),
                                     inc="T" if c == NC_D - 1 else None)
                    hsl = self.ar1[:, fc * 2048 + tc * 512: fc * 2048 + (tc + 1) * 512]
                    rtok = self.op(s, "activation", hsl, psA, AF.Relu,
                                   waits=_clean([tm] + (h_guard if fc == 0 and tc == 0 else [])),
                                   inc="S")
                    sq_last = self.op(v, "tensor_mul", hsl, hsl, hsl,
                                      waits=(rtok,), inc="V")
                    abank[ai % 4] = sq_last
                    ai += 1
                if first:
                    wubf = d["wu_bf"][e, :, fc * 128:(fc + 1) * 128] if e < 8 \
                        else d["swu_bf"][:, fc * 128:(fc + 1) * 128]
                    d3 = wubf.rearrange("(c p) f -> p c f", p=128)
                    s3 = wudst.rearrange("p (c f) -> p c f", f=128)
                    sttok = self.dma("sync", d3, s3, waits=(wutok,),
                                     sem="dSt" + str(par))
                    wu_guard[par] = [tm, sttok]
                else:
                    wu_guard[par] = [tm]
            for tt in range(NT):
                for dh in range(2):
                    psY = self.p2[:, (yi % 2) * 512:(yi % 2) * 512 + 512]
                    wts = _clean((sq_last, wdtok, ybank[yi % 2]))
                    tm = None
                    for fc in range(11):
                        tm = self.op(t, "matmul", psY,
                                     self.ar1[:, fc * 2048 + tt * 128: fc * 2048 + tt * 128 + 128],
                                     self.ar2[:, fc * 1024 + dh * 512: fc * 1024 + dh * 512 + 512],
                                     start=(fc == 0), stop=(fc == 10),
                                     waits=wts if fc == 0 else (),
                                     inc="T" if fc == 10 else None)
                    b_last = tm
                    xsl = self.xr[:, tt * 1024 + dh * 512: tt * 1024 + dh * 512 + 512]
                    scl = (self.comb[:, tt * 8 + e: tt * 8 + e + 1] if e < 8 else 0.5)
                    last_evac = self.op(v, "scalar_tensor_tensor", xsl, psY,
                                        scl, xsl, ALU.mult, ALU.add,
                                        waits=_clean((tm,)),
                                        fwaits=(last_gate,) if e < 8 else (),
                                        inc="V")
                    ybank[yi % 2] = last_evac
                    yi += 1
            h_guard = [b_last]
            wd_guard = [b_last] + ([self.dma_done("dStW2")] if wd_store else [])
        self.setg("ar1", b_last)
        self.setg("ar2", b_last,
                  self.dma_done("dStW2") if first else None)
        self.setg("ar3", *(wu_guard[0] + wu_guard[1] + wu_guard[2] + wu_guard[3]))
        self.setg("xr", last_evac)
        self.setg("comb", last_evac)
        self.setg("p0p1", *[abank[i] for i in range(4)])
        self.set2("p2", 0, ybank[0])
        self.set2("p2", 1, ybank[1])

    def barrier(self):
        toks = []
        for eng, sem in (("tensor", "T"), ("vector", "V"), ("scalar", "S")):
            toks.append(self.op(eng, "nop", inc=sem))
        for s in self.DMA_SEMS:
            toks.append(self.dma_done(s))
        for eng in ("tensor", "vector", "scalar", "gpsimd", "sync"):
            for tk in toks:
                self.wait_force(eng, tk)

    def emit_epilogue(self, d):
        xr_done = _clean(self.use("xr"))
        xo3 = d["xout"].rearrange("(c p) f -> p c f", p=128)
        xr3 = self.xr[:, :].rearrange("p (c f) -> p c f", f=1024)
        for i in range(4):
            self.dma("sync", xo3[:, i * 4:(i + 1) * 4, :],
                     xr3[:, i * 4:(i + 1) * 4, :], waits=xr_done)
        self.wait("sync", self.dma_done("dX"))

    def build(self, d):
        self.emit_prologue(d, "F0" in self.layers, "F1" in self.layers)
        f0_seen = 0
        f1_seen = 0
        for l in self.layers:
            if l == "F0":
                self.emit_f0(d, first=(f0_seen == 0))
                f0_seen += 1
            elif l == "A0":
                self.emit_attn(d, "A0")
            elif l == "A1":
                self.emit_attn(d, "A1")
            elif l == "F1":
                self.emit_f1(d, first=(f1_seen == 0))
                f1_seen += 1
        self.emit_epilogue(d)
        with self.nc.Block() as blk:
            for name, reg in (("tensor", blk.tensor), ("vector", blk.vector),
                              ("scalar", blk.scalar), ("gpsimd", blk.gpsimd),
                              ("sync", blk.sync)):
                ops = self.prog[name]

                def body(e, ops=ops):
                    for f in ops:
                        f(e)
                reg(body)


def _declare_inputs(nc, layers):
    d = {}

    def dt(name, shape, dtype=F32, kind="ExternalInput"):
        d[name] = nc.dram_tensor(name, shape, dtype, kind=kind).ap()
    dt("xin", [2048, 1024])
    dt("xout", [2048, 1024], kind="ExternalOutput")
    dt("ident", [128, 128], BF16)
    dt("ones_bf", [1, 512], BF16)
    dt("ones_f", [1, 128], F32)
    if "F0" in layers:
        dt("rw_t", [1024, 8])
        dt("esc_row", [1, 8])
        dt("w1_t", [8, 1024, 4096])
        dt("w2_t", [8, 4096, 1024])
        dt("w1_bf", [8, 1024, 4096], BF16, kind="Internal")
        dt("w2_bf", [8, 4096, 1024], BF16, kind="Internal")
    if "A0" in layers:
        dt("wq_t", [1024, 1024])
        dt("wk_t", [1024, 1024])
        dt("wv_t", [1024, 1024])
        dt("wo_t", [1024, 1024])
        dt("we_t", [1024, 128])
    if "A1" in layers:
        dt("qkv_w", [1024, 3072])
        dt("proj_w", [1024, 1024])
    if "A1" in layers:
        dt("qkv_b", [1, 3072])
        dt("proj_b", [1, 1024])
    if "F1" in layers:
        dt("gate_t", [1024, 8])
        dt("eb_row", [1, 8])
        dt("wu_t", [8, 1024, 1408])
        dt("wd_t", [8, 1408, 1024])
        dt("swu_t", [1024, 1408])
        dt("swd_t", [1408, 1024])
        dt("wu_bf", [8, 1024, 1408], BF16, kind="Internal")
        dt("wd_bf", [8, 1408, 1024], BF16, kind="Internal")
        dt("swu_bf", [1024, 1408], BF16, kind="Internal")
        dt("swd_bf", [1408, 1024], BF16, kind="Internal")
    return d


def build_nc(layers, tag=""):
    nc = bass.Bass("TRN2", target_bir_lowering=False, debug=False,
                   dynamic_dma_scratch_size=16384)
    d = _declare_inputs(nc, layers)
    mk = MK(nc, layers, tag=tag)
    ctx = ExitStack()
    with ctx:
        mk.alloc(ctx, has_l1=("A1" in layers or "F1" in layers))
        mk.build(d)
    return nc


def _host_weights(inputs):
    c = np.ascontiguousarray
    w = {}
    w["rw_t"] = c(inputs["L0_ffn_router_w"].T)
    w["esc_row"] = c(inputs["L0_ffn_expert_scale"].reshape(1, 8))
    w["w1_t"] = c(inputs["L0_ffn_w1"].transpose(0, 2, 1))
    w["w2_t"] = c(inputs["L0_ffn_w2"].transpose(0, 2, 1))
    w["wq_t"] = c(inputs["L0_attn_Wq"].T)
    w["wk_t"] = c(inputs["L0_attn_Wk"].T)
    w["wv_t"] = c(inputs["L0_attn_Wv"].T)
    w["wo_t"] = c(inputs["L0_attn_Wo"].T)
    w["we_t"] = c(inputs["L0_attn_We"].T)
    w["qkv_w"] = c(inputs["L1_attn_qkv_w"])
    w["qkv_b"] = c(inputs["L1_attn_qkv_b"].reshape(1, 3072))
    w["proj_w"] = c(inputs["L1_attn_proj_w"])
    w["proj_b"] = c(inputs["L1_attn_proj_b"].reshape(1, 1024))
    w["gate_t"] = c(inputs["L1_ffn_gate_w"].T)
    w["eb_row"] = c(inputs["L1_ffn_expert_bias"].reshape(1, 8))
    w["wu_t"] = c(inputs["L1_ffn_w_up"].transpose(0, 2, 1))
    w["wd_t"] = c(inputs["L1_ffn_w_down"].transpose(0, 2, 1))
    w["swu_t"] = c(inputs["L1_ffn_shared_w_up"].T)
    w["swd_t"] = c(inputs["L1_ffn_shared_w_down"].T)
    import ml_dtypes
    w["ident"] = np.eye(128, dtype=ml_dtypes.bfloat16)
    w["ones_bf"] = np.ones((1, 512), dtype=ml_dtypes.bfloat16)
    w["ones_f"] = np.ones((1, 128), dtype=np.float32)
    return w


def _needed_names(layers):
    names = ["ident", "ones_bf", "ones_f"]
    if "F0" in layers:
        names += ["rw_t", "esc_row", "w1_t", "w2_t"]
    if "A0" in layers:
        names += ["wq_t", "wk_t", "wv_t", "wo_t", "we_t"]
    if "A1" in layers:
        names += ["qkv_w", "proj_w", "qkv_b", "proj_b"]
    if "F1" in layers:
        names += ["gate_t", "eb_row", "wu_t", "wd_t", "swu_t", "swd_t"]
    return names


def run_layers(x, inputs, layers, n_cores=8):
    w = _host_weights(inputs)
    nc = build_nc(layers)
    need = _needed_names(layers)
    in_maps = []
    for cidx in range(n_cores):
        m = {"xin": np.ascontiguousarray(x[cidx])}
        for k in need:
            m[k] = w[k]
        in_maps.append(m)
    res = run_bass_kernel_spmd(nc, in_maps, core_ids=list(range(n_cores)))
    return np.stack([np.asarray(r["xout"]) for r in res.results], axis=0)


def kernel(**inputs):
    x = np.ascontiguousarray(np.asarray(inputs["L0_attn_X"], dtype=np.float32))
    out = run_layers(x, inputs, list(ALL_LAYERS), n_cores=8)
    return out.astype(np.float32)
